# version 10 (final)
# baseline (speedup 1.0000x reference)
import numpy as np
import ml_dtypes
import concourse.bass as bass
import concourse.mybir as mybir
from concourse.bass_utils import run_bass_kernel_spmd

F32 = mybir.dt.float32
BF16 = mybir.dt.bfloat16
AF = mybir.ActivationFunctionType
ALU = mybir.AluOpType
AX = mybir.AxisListType

B_LOC = 4
T = 2048
D = 1024
H = 16
DH = 64
C = 128
N_TILE = 16
TILES_PER_FFN = 4
N_FFN = N_TILE // TILES_PER_FFN
NE = 8
DFF = 4096
NG = 16
SKIP_ATTN = False
SKIP_FFN = False
SKIP_DN = False

ENGS = ("sy", "gp", "pe", "dv", "ac")


class Tr:
    __slots__ = ("w", "r", "name")

    def __init__(self, name=""):
        self.w = {}
        self.r = {}
        self.name = name


class Op:
    __slots__ = ("eng", "fn", "deps", "hazard", "dma", "inc", "seq",
                 "sem_after", "waits", "tag")

    def __init__(self, eng, fn, hazard, dma, tag):
        self.eng = eng
        self.fn = fn
        self.deps = []
        self.hazard = hazard
        self.dma = dma
        self.inc = False
        self.seq = -1
        self.sem_after = 0
        self.waits = None
        self.tag = tag


class KB:
    def __init__(self, nc):
        self.nc = nc
        self.ops = []

    def op(self, eng, fn, reads=(), writes=(), deps=(), hazard=False,
           dma=False, tag=""):
        o = Op(eng, fn, hazard, dma, tag)
        dd = list(deps)
        for tr in reads:
            dd.extend(tr.w.values())
        for tr in writes:
            dd.extend(tr.w.values())
            dd.extend(tr.r.values())
        o.deps = dd
        for tr in reads:
            tr.r[eng] = o
        for tr in writes:
            tr.w = {eng: o}
            tr.r = {}
        self.ops.append(o)
        return o

    def fence(self, engs=("pe", "dv", "ac")):
        last = {}
        for o in self.ops:
            if o.eng in engs:
                last[o.eng] = o
        return list(last.values())

    def finalize_and_emit(self, sems):
        for o in self.ops:
            for d in o.deps:
                if d is o:
                    continue
                if d.eng == o.eng and not d.dma and not d.hazard:
                    continue
                d.inc = True
        cnt = {e: 0 for e in ENGS}
        for o in self.ops:
            if o.dma:
                o.inc = True
            if o.inc:
                cnt[o.eng] += 16 if o.dma else 1
                o.sem_after = cnt[o.eng]
        prev_dma = {e: None for e in ENGS}
        prev_dma_at = []
        for o in self.ops:
            prev_dma_at.append(dict(prev_dma))
            if o.dma:
                prev_dma[o.eng] = o
        for idx, o in enumerate(self.ops):
            w = {}
            for d in o.deps:
                if d is o:
                    continue
                if d.eng == o.eng and not d.dma and not d.hazard:
                    continue
                if d.dma:
                    lat = prev_dma_at[idx][d.eng]
                    v = lat.sem_after if lat is not None else d.sem_after
                    assert v >= d.sem_after
                else:
                    v = d.sem_after
                assert v > 0, f"dep without inc: {d.tag} <- {o.tag}"
                if w.get(d.eng, 0) < v:
                    w[d.eng] = v
            o.waits = w
        self.totals = cnt

        ops_by_eng = {e: [o for o in self.ops if o.eng == e] for e in ENGS}

        ptr = {e: 0 for e in ENGS}
        sem = {e: 0 for e in ENGS}
        n_done = 0
        n_total = len(self.ops)
        while n_done < n_total:
            progressed = False
            for e in ENGS:
                while ptr[e] < len(ops_by_eng[e]):
                    o = ops_by_eng[e][ptr[e]]
                    if any(sem[se] < v for se, v in o.waits.items()):
                        break
                    if o.inc:
                        sem[e] = o.sem_after
                    ptr[e] += 1
                    n_done += 1
                    progressed = True
            if not progressed:
                stuck = {e: (ops_by_eng[e][ptr[e]].tag, ops_by_eng[e][ptr[e]].waits)
                         for e in ENGS if ptr[e] < len(ops_by_eng[e])}
                raise RuntimeError(f"deadlock in plan: {stuck}, sems={sem}")

        with self.nc.Block() as blk:
            def make_body(ename):
                def body(eng):
                    hw = {e: 0 for e in ENGS}
                    for o in ops_by_eng[ename]:
                        for se, val in sorted(o.waits.items()):
                            if val > hw[se]:
                                eng.wait_ge(sems[se], val)
                                hw[se] = val
                        inst = o.fn(eng)
                        if o.inc:
                            inst.then_inc(sems[o.eng], 16 if o.dma else 1)
                return body

            blk.sync(make_body("sy"))
            blk.gpsimd(make_body("gp"))
            blk.tensor(make_body("pe"))
            blk.vector(make_body("dv"))
            blk.scalar(make_body("ac"))


class PsumPool:
    def __init__(self, tensors, names):
        self.slots = [{"t": t, "tr": Tr(n), "stamp": i}
                      for i, (t, n) in enumerate(zip(tensors, names))]
        self.clock = len(tensors)

    def alloc(self, subset=None):
        cand = self.slots if subset is None else [self.slots[i] for i in subset]
        s = min(cand, key=lambda q: q["stamp"])
        s["stamp"] = self.clock
        self.clock += 1
        return s["t"], s["tr"]


class Arena:
    def __init__(self, nc, total_bytes, ctx):
        self.nc = nc
        self.guard = ctx.enter_context(
            nc.sbuf_tensor("arena_guard", [128, total_bytes], mybir.dt.uint8))
        self.base = nc.lookup_mloc(self.guard).addr
        assert self.base % 32 == 0, self.base

    def at(self, name, offset, shape, dtype):
        return self.nc.alloc_sbuf_tensor_at(name, list(shape), dtype,
                                            offset=self.base + offset)


def build_nc():
    nc = bass.Bass("TRN2", target_bir_lowering=False)

    x_d = nc.dram_tensor("x", [B_LOC, T, D], F32, kind="ExternalInput").ap()
    out_d = nc.dram_tensor("out", [B_LOC, T, D], F32, kind="ExternalOutput").ap()
    wq_d = nc.dram_tensor("wq", [D, D], F32, kind="ExternalInput").ap()
    wk_d = nc.dram_tensor("wk", [D, D], F32, kind="ExternalInput").ap()
    wv_d = nc.dram_tensor("wv", [D, D], F32, kind="ExternalInput").ap()
    wbeta_d = nc.dram_tensor("wbeta", [D, H], F32, kind="ExternalInput").ap()
    wout_d = nc.dram_tensor("wout", [D, D], F32, kind="ExternalInput").ap()
    w1_d = nc.dram_tensor("w1", [NE, D, DFF], F32, kind="ExternalInput").ap()
    w2_d = nc.dram_tensor("w2", [NE, DFF, D], F32, kind="ExternalInput").ap()
    cstf_d = nc.dram_tensor("cst_f", [128, 257], F32, kind="ExternalInput").ap()
    cstb_d = nc.dram_tensor("cst_b", [128, 1224], F32, kind="ExternalInput").ap()
    w1s_d = nc.dram_tensor("w1stage", [NE, 2, 128, 16384], BF16,
                           kind="Internal").ap()
    w2s_d = nc.dram_tensor("w2stage", [NE, 2, 128, 16384], BF16,
                           kind="Internal").ap()
    was_d = nc.dram_tensor("wastage", [4, 128, 8192], BF16,
                           kind="Internal").ap()

    ident_bf = nc.inline_tensor(np.eye(128, dtype=ml_dtypes.bfloat16), "identbf").ap()
    ident_f = nc.inline_tensor(np.eye(128, dtype=np.float32), "identf").ap()
    triu1 = np.triu(np.ones((128, 128), dtype=np.float32), 1)
    trii = np.triu(np.ones((128, 128), dtype=np.float32), 0)
    triu1_d = nc.inline_tensor(triu1.astype(ml_dtypes.bfloat16), "triu1c").ap()
    trii_d = nc.inline_tensor(trii.astype(ml_dtypes.bfloat16), "triic").ap()

    from contextlib import ExitStack
    es = ExitStack()
    with es:
        off = 0

        def nxt(nbytes, align=32):
            nonlocal off
            off = (off + align - 1) // align * align
            o = off
            off += nbytes
            return o

        TOTAL = 212832
        ar = Arena(nc, TOTAL, es)
        A = ar.at

        o_identbf = nxt(256); o_identf = nxt(512)
        o_triu1 = nxt(256); o_trii = nxt(256)
        o_cstf = nxt(1028)
        o_cstb = nxt(2448)
        o_S = nxt(8192)
        o_Sbf = nxt(4096)
        o_wcols = nxt(512)
        o_wfm = nxt(4096)
        o_btm = nxt(256)
        o_bnb = nxt(48)
        o_cols = nxt(64)
        o_gcols = nxt(64)
        o_gp = nxt(32)
        o_gind = nxt(32)
        o_x1 = nxt(65536)
        o_xn2 = nxt(32768)
        o_arena = nxt(0)
        o_xntm = o_arena + 0
        o_xnfm = o_arena + 2048
        o_qfm = o_arena + 10240
        o_ktm = o_arena + 18432
        o_vtm = o_arena + 26624
        o_wbufA = o_arena + 34816
        o_wbufB = o_arena + 51200
        o_wbetabf = o_arena + 67584
        o_attsc = o_arena + 67840
        o_bsig = o_arena + 69888
        o_rnorm = o_arena + 71936
        aoff = o_arena + 72192
        o_knfm = []
        o_P = []
        o_Q = []
        o_R = []
        o_Ttr = []
        o_Upr = []
        o_wfmw = []
        for _ws in range(2):
            o_knfm.append(aoff); aoff += 1024
            for _k in range(2):
                o_P.append(aoff); aoff += 1024
                o_Q.append(aoff); aoff += 1024
                o_R.append(aoff); aoff += 1024
            o_Ttr.append(aoff); aoff += 1024
            o_Upr.append(aoff); aoff += 512
            o_wfmw.append(aoff); aoff += 1024
        o_hg = o_arena + 0
        o_w1h = o_arena + 16384
        o_w2h = o_arena + 49152
        assert max(aoff, o_arena + 81920) <= TOTAL, (aoff, TOTAL)

        identbf = A("identbf_s", o_identbf, [128, 128], BF16)
        identf = A("identf_s", o_identf, [128, 128], F32)
        triu1s = A("triu1_s", o_triu1, [128, 128], BF16)
        triis = A("trii_s", o_trii, [128, 128], BF16)
        cstf = A("cstf_s", o_cstf, [128, 257], F32)
        cstb = A("cstb_s", o_cstb, [128, 1224], BF16)
        bbetas = cstf[0:16, 0:1]
        b2bf = cstb[0:8, 0:1024]
        bgbf = cstb[0:1, 1024:1032]
        oness = cstb[0:1, 1032:1160]
        S32 = A("S32", o_S, [128, 2048], F32)
        Sbf = A("Sbf", o_Sbf, [128, 2048], BF16)
        wcols = A("wcols", o_wcols, [128, 128], F32)
        wfm = A("wfm", o_wfm, [8, 2048], BF16)
        bsig = A("bsig", o_bsig, [16, 512], F32)
        btm = A("btm", o_btm, [128, 4, 16], F32)
        bnb = A("bnb", o_bnb, [128, 12], F32)
        cols = A("colsx", o_cols, [128, 16], F32)
        gcols = A("gcols", o_gcols, [128, 16], F32)
        gp_t = A("gp_t", o_gp, [128, 8], F32)
        gind = A("gind", o_gind, [128, 8], F32)
        x1 = A("x1", o_x1, [128, 16, 1024], F32)
        xn2 = A("xn2", o_xn2, [128, 8, 2048], BF16)
        xntm = A("xntm", o_xntm, [128, 1024], BF16)
        xnfm = A("xnfm", o_xnfm, [128, 8, 512], BF16)
        qfm = A("qfm", o_qfm, [128, 8, 512], BF16)
        ktm = A("ktm", o_ktm, [128, 4, 1024], BF16)
        vtm = A("vtm", o_vtm, [128, 4, 1024], BF16)
        wbufA = A("wbufA", o_wbufA, [128, 8, 1024], BF16)
        wbufB = A("wbufB", o_wbufB, [128, 8, 1024], BF16)
        wbetabf = A("wbetabf", o_wbetabf, [128, 8, 16], BF16)
        attsc = A("attsc", o_attsc, [128, 512], F32)
        rnorm = A("rnorm", o_rnorm, [128, 64], F32)
        knfm_s = [A(f"knfm{w}", o_knfm[w], [128, 512], BF16) for w in range(2)]
        Pb_s = [[A(f"Pb{w}{k}", o_P[w * 2 + k], [128, 512], BF16)
                 for k in range(2)] for w in range(2)]
        Qb_s = [[A(f"Qb{w}{k}", o_Q[w * 2 + k], [128, 512], BF16)
                 for k in range(2)] for w in range(2)]
        Rb_s = [[A(f"Rb{w}{k}", o_R[w * 2 + k], [128, 512], BF16)
                 for k in range(2)] for w in range(2)]
        ttrs_s = [A(f"ttrs{w}", o_Ttr[w], [128, 512], BF16) for w in range(2)]
        uprs_s = [A(f"uprs{w}", o_Upr[w], [128, 256], BF16) for w in range(2)]
        wfmw_s = [A(f"wfmw{w}", o_wfmw[w], [128, 512], BF16) for w in range(2)]
        hg = A("hg", o_hg, [128, 16, 512], BF16)
        w1h = A("w1h", o_w1h, [128, 8, 2048], BF16)
        w2h = A("w2h", o_w2h, [128, 16, 1024], BF16)

        psf = [es.enter_context(nc.psum_tensor(f"psf{i}", [128, 512], F32))
               for i in range(6)]
        psb = [es.enter_context(nc.psum_tensor(f"psb{i}", [128, 1024], BF16))
               for i in range(2)]
        sems = {e: es.enter_context(nc.semaphore(f"sem_{e}")) for e in ENGS}

        kb = KB(nc)
        fpool = PsumPool(psf, [f"psf{i}" for i in range(6)])
        bpool = PsumPool(psb, [f"psb{i}" for i in range(2)])

        t_consts = Tr("consts")
        t_x1 = [Tr(f"x1_{g}") for g in range(NG)]
        t_xn2 = [Tr("xn2_lo"), Tr("xn2_hi")]
        t_xntm = Tr("xntm")
        t_xnfm = Tr("xnfm")
        t_qfm = [Tr(f"qfm{i}") for i in range(8)]
        t_ktm = [Tr(f"ktm{g}") for g in range(4)]
        t_vtm = [Tr(f"vtm{g}") for g in range(4)]
        t_wbufA = Tr("wbufA")
        t_wbufB = Tr("wbufB")
        t_wbetabf = Tr("wbetabf")
        t_bsig = Tr("bsig")
        t_btm = [Tr(f"btm{g}") for g in range(4)]
        t_bnb = Tr("bnb")
        t_cols = Tr("cols")
        t_gcols = Tr("gcols")
        t_gp = Tr("gp")
        t_gind = Tr("gind")
        t_S32 = [Tr(f"S32_{p}") for p in range(64)]
        t_Sbf = [Tr(f"Sbf_{p}") for p in range(64)]
        t_wcols = Tr("wcols")
        t_wfm = Tr("wfm")
        t_attsc = Tr("attsc")
        t_rnorm = [Tr(f"rnorm{g}") for g in range(4)]
        t_knfm_s = [Tr(f"knfm{w}") for w in range(2)]
        t_P_s = [[Tr(f"P{w}{k}") for k in range(2)] for w in range(2)]
        t_Q_s = [[Tr(f"Q{w}{k}") for k in range(2)] for w in range(2)]
        t_R_s = [[Tr(f"R{w}{k}") for k in range(2)] for w in range(2)]
        t_ttr_s = [Tr(f"ttr{w}") for w in range(2)]
        t_upr_s = [Tr(f"upr{w}") for w in range(2)]
        t_wfmw_s = [Tr(f"wfmw{w}") for w in range(2)]
        t_hg = Tr("hg")
        t_w1h = Tr("w1h")
        t_w2h = Tr("w2h")
        t_cstf = Tr("cstf")
        t_cstb = Tr("cstb")

        def dma(q, dst, src, tag, reads=(), writes=(), deps=(), mld=None):
            if mld is None:
                return kb.op(q, lambda e, d=dst, s=src:
                             e.dma_start(out=d, in_=s),
                             reads=reads, writes=writes, deps=deps, dma=True,
                             tag=tag)
            return kb.op(q, lambda e, d=dst, s=src, m=mld:
                         e.dma_start(out=d, in_=s, max_dma_last_dim=m),
                         reads=reads, writes=writes, deps=deps, dma=True,
                         tag=tag)

        dma("sy", identbf[:, :], ident_bf[:, :], "ldidb", writes=[t_consts])
        dma("sy", identf[:, :], ident_f[:, :], "ldidf", writes=[t_consts])
        dma("sy", triu1s[:, :], triu1_d[:, :], "ldtr1", writes=[t_consts])
        dma("sy", triis[:, :], trii_d[:, :], "ldtri", writes=[t_consts])
        dma("sy", cstf[:, :], cstf_d[:, :], "ldcf", writes=[t_cstf])
        dma("gp", cstb[:, :], cstb_d[:, :], "ldcb", writes=[t_cstb])

        ptbg, ptrbg = fpool.alloc()
        kb.op("pe", lambda e, ptbg=ptbg:
              e.matmul(ptbg[0:128, 0:8], oness[:, :], bgbf[:, :],
                       start=True, stop=True),
              reads=[t_cstb], writes=[ptrbg], tag="bgmm")
        kb.op("dv", lambda e, ptbg=ptbg:
              e.tensor_copy(cols[:, 8:16], ptbg[0:128, 0:8]),
              reads=[ptrbg], writes=[t_cols], hazard=True, tag="bgev")

        kb.op("dv", lambda e: e.memset(S32[:, :], 0.0),
              writes=list(t_S32), hazard=True, tag="S0")
        kb.op("ac", lambda e: e.copy(Sbf[:, :], S32[:, :]),
              reads=list(t_S32), writes=list(t_Sbf), tag="S0bf")

        def emit_ln(gg, dst_fm, dst_trs, dst_off, tag):
            xsl = x1[:, gg, :]
            src_tr = t_x1[gg]
            kb.op("dv", lambda e: e.bn_stats(bnb[:, 0:6], xsl[:, 0:512]),
                  reads=[src_tr], writes=[t_bnb], hazard=True, tag=f"{tag}b0")
            kb.op("dv", lambda e: e.bn_stats(bnb[:, 6:12], xsl[:, 512:1024]),
                  reads=[src_tr], writes=[t_bnb], hazard=True, tag=f"{tag}b1")
            kb.op("dv", lambda e: e.bn_aggr(cols[:, 1:3], bnb[:, 0:12]),
                  reads=[t_bnb], writes=[t_cols], hazard=True, tag=f"{tag}ba")
            kb.op("dv", lambda e: e.tensor_scalar(cols[:, 3:4], cols[:, 2:3],
                                                  1e-5, None, ALU.add),
                  reads=[t_cols], writes=[t_cols], hazard=True, tag=f"{tag}ve")
            kb.op("ac", lambda e: e.activation(cols[:, 4:5], cols[:, 3:4], AF.Ln),
                  reads=[t_cols], writes=[t_cols], hazard=True, tag=f"{tag}ln")
            kb.op("dv", lambda e: e.tensor_scalar(cols[:, 5:6], cols[:, 4:5],
                                                  -0.5, None, ALU.mult),
                  reads=[t_cols], writes=[t_cols], hazard=True, tag=f"{tag}mh")
            kb.op("ac", lambda e: e.activation(cols[:, 6:7], cols[:, 5:6], AF.Exp),
                  reads=[t_cols], writes=[t_cols], hazard=True, tag=f"{tag}ex")
            kb.op("dv", lambda e: e.tensor_scalar(xntm[:, :], xsl, cols[:, 1:2],
                                                  cols[:, 6:7], ALU.subtract,
                                                  ALU.mult),
                  reads=[src_tr, t_cols], writes=[t_xntm], tag=f"{tag}ap")
            for half in range(2):
                pt, ptr = bpool.alloc()
                for k in range(4):
                    dc = half * 4 + k
                    kb.op("pe", lambda e, pt=pt, dc=dc, k=k:
                          e.transpose(pt[:, k * 128:(k + 1) * 128],
                                      xntm[:, dc * 128:(dc + 1) * 128],
                                      identbf[:, :]),
                          reads=[t_xntm, t_consts], writes=[ptr], tag=f"{tag}tp")
                kb.op("ac", lambda e, pt=pt, half=half:
                      e.copy(dst_fm[:, half * 4:(half + 1) * 4,
                                    dst_off:dst_off + 128],
                             pt[:, 0:512].rearrange("p (k t) -> p k t", k=4)),
                      reads=[ptr], writes=dst_trs, tag=f"{tag}te")

        def wload(dst, tr, src2d, tag, extra=(), sidx=None, tt=None):
            if sidx is None or tt == 0:
                srcr = src2d.rearrange("(dc p) j -> p dc j", p=128)
                dma("gp", dst[:, 0:4, :], srcr[:, 0:4, :], tag + "a",
                    writes=[tr], deps=extra)
                dma("gp", dst[:, 4:8, :], srcr[:, 4:8, :], tag + "b",
                    writes=[tr], deps=extra)
                if sidx is not None:
                    dma("sy", was_d[sidx], dst[:, :, :], tag + "s", reads=[tr])
            else:
                dma("sy", dst[:, :, :], was_d[sidx], tag + "l",
                    writes=[tr], deps=extra)

        def emit_wave(g, hs, tt, ws):
            knfm = knfm_s[ws]
            Pb = Pb_s[ws]
            Qb = Qb_s[ws]
            Rb = Rb_s[ws]
            ttrs = ttrs_s[ws]
            uprs = uprs_s[ws]
            wfmw = wfmw_s[ws]
            t_knfm = t_knfm_s[ws]
            t_P = t_P_s[ws]
            t_Q = t_Q_s[ws]
            t_R = t_R_s[ws]
            t_ttr = t_ttr_s[ws]
            t_upr = t_upr_s[ws]
            t_wfmw = t_wfmw_s[ws]
            par = hs[0] % 2
            base = 64 * par
            pis = [(g * 16 + h) // 2 for h in hs]
            ics = [h // 2 for h in hs]
            kpar = ktm[:, g, :].rearrange("p (pr two f) -> p pr two f",
                                          two=2, f=64)[:, ics[0]:ics[0] + 4,
                                                       par, :]
            vpar = vtm[:, g, :].rearrange("p (pr two f) -> p pr two f",
                                          two=2, f=64)[:, ics[0]:ics[0] + 4,
                                                       par, :]

            pt, ptr = bpool.alloc()
            for i in range(4):
                kb.op("pe", lambda e, pt=pt, i=i, h=hs[i]:
                      e.transpose(pt[base:base + 64, i * 128:(i + 1) * 128],
                                  ktm[:, g, h * 64:(h + 1) * 64],
                                  identbf[:, :]),
                      reads=[t_ktm[g], t_consts], writes=[ptr], tag="dKt")
            kb.op("ac", lambda e, pt=pt:
                  e.copy(knfm[base:base + 64, 0:512], pt[base:base + 64, 0:512]),
                  reads=[ptr], writes=[t_knfm], tag="dKe")
            yield

            def knsl(i):
                return knfm[base:base + 64, i * 128:(i + 1) * 128]

            ptG, ptrG = fpool.alloc()
            for i in range(4):
                kb.op("pe", lambda e, ptG=ptG, i=i:
                      e.matmul(ptG[:, i * 128:(i + 1) * 128], knsl(i), knsl(i),
                               start=True, stop=True),
                      reads=[t_knfm], writes=[ptrG], tag="dG")
            for i in range(4):
                kb.op("dv", lambda e, ptG=ptG, i=i, h=hs[i]:
                      e.scalar_tensor_tensor(Qb[0][:, i * 128:(i + 1) * 128],
                                             ptG[:, i * 128:(i + 1) * 128],
                                             btm[:, g, h:h + 1],
                                             triu1s[:, 0:128],
                                             ALU.mult, ALU.mult),
                      reads=[ptrG, t_btm[g], t_consts], writes=[t_Q[0]],
                      tag="dBt")
            yield

            ptP, ptrP = bpool.alloc()
            for i in range(4):
                kb.op("pe", lambda e, ptP=ptP, i=i:
                      e.transpose(ptP[:, i * 128:(i + 1) * 128],
                                  Qb[0][:, i * 128:(i + 1) * 128],
                                  identbf[:, :]),
                      reads=[t_Q[0], t_consts], writes=[ptrP], tag="dPt")
            kb.op("ac", lambda e, ptP=ptP: e.copy(Pb[0][:, :], ptP[:, 0:512]),
                  reads=[ptrP], writes=[t_P[0]], tag="dPe")
            yield

            ptA, ptrA = fpool.alloc()
            for i in range(4):
                h = hs[i]
                kb.op("pe", lambda e, ptA=ptA, i=i, h=h:
                      e.matmul(ptA[:, i * 128:i * 128 + 64],
                               Qb[0][:, i * 128:(i + 1) * 128],
                               ktm[:, g, h * 64:(h + 1) * 64],
                               start=True, stop=True),
                      reads=[t_Q[0], t_ktm[g]], writes=[ptrA], tag="dA0k")
                kb.op("pe", lambda e, ptA=ptA, i=i, h=h:
                      e.matmul(ptA[:, i * 128 + 64:(i + 1) * 128],
                               Qb[0][:, i * 128:(i + 1) * 128],
                               vtm[:, g, h * 64:(h + 1) * 64],
                               start=True, stop=True),
                      reads=[t_Q[0], t_vtm[g]], writes=[ptrA], tag="dA0v")
            kb.op("dv", lambda e, ptA=ptA:
                  e.tensor_sub(
                      Rb[0][:, :].rearrange("p (c f) -> p c f", c=4)[:, :, 0:64],
                      kpar,
                      ptA[:, 0:512].rearrange("p (c f) -> p c f", c=4)[:, :, 0:64]),
                  reads=[ptrA, t_ktm[g]], writes=[t_R[0]], tag="dA0sk")
            kb.op("dv", lambda e, ptA=ptA:
                  e.tensor_sub(
                      Rb[0][:, :].rearrange("p (c f) -> p c f", c=4)[:, :, 64:128],
                      vpar,
                      ptA[:, 0:512].rearrange("p (c f) -> p c f", c=4)[:, :, 64:128]),
                  reads=[ptrA, t_vtm[g]], writes=[t_R[0]], tag="dA0sv")
            yield

            rcur, pcur, qcur = 0, 0, 0
            for j in range(1, 7):
                pn, qn = 1 - pcur, 1 - qcur
                if j < 6:
                    ptS, ptrS = fpool.alloc()
                    for i in range(4):
                        kb.op("pe", lambda e, ptS=ptS, i=i, qc=qcur, pc=pcur:
                              e.matmul(ptS[:, i * 128:(i + 1) * 128],
                                       Qb[qc][:, i * 128:(i + 1) * 128],
                                       Pb[pc][:, i * 128:(i + 1) * 128],
                                       start=True, stop=True),
                              reads=[t_Q[qcur], t_P[pcur]], writes=[ptrS],
                              tag="dSP")
                    kb.op("ac", lambda e, ptS=ptS, pn=pn:
                          e.copy(Pb[pn][:, :], ptS[:, 0:512]),
                          reads=[ptrS], writes=[t_P[pn]], tag="dSPe")
                    yield
                ptS2, ptrS2 = fpool.alloc()
                for i in range(4):
                    kb.op("pe", lambda e, ptS2=ptS2, i=i, qc=qcur, pc=pcur:
                          e.matmul(ptS2[:, i * 128:(i + 1) * 128],
                                   Pb[pc][:, i * 128:(i + 1) * 128],
                                   Qb[qc][:, i * 128:(i + 1) * 128],
                                   start=True, stop=True),
                          reads=[t_Q[qcur], t_P[pcur]], writes=[ptrS2],
                          tag="dSQ")
                kb.op("ac", lambda e, ptS2=ptS2, qn=qn:
                      e.copy(Qb[qn][:, :], ptS2[:, 0:512]),
                      reads=[ptrS2], writes=[t_Q[qn]], tag="dSQe")
                yield
                ptA2, ptrA2 = fpool.alloc()
                rn = 1 - rcur
                for i in range(4):
                    kb.op("pe", lambda e, ptA2=ptA2, i=i, qn=qn, rc=rcur:
                          e.matmul(ptA2[:, i * 128:(i + 1) * 128],
                                   Qb[qn][:, i * 128:(i + 1) * 128],
                                   Rb[rc][:, i * 128:(i + 1) * 128],
                                   start=True, stop=True),
                          reads=[t_Q[qn], t_R[rcur]], writes=[ptrA2], tag="dAp")
                kb.op("dv", lambda e, ptA2=ptA2, rn=rn, rc=rcur:
                      e.tensor_add(Rb[rn][:, :], Rb[rc][:, :], ptA2[:, 0:512]),
                      reads=[ptrA2, t_R[rcur]], writes=[t_R[rn]], tag="dApe")
                yield
                rcur = rn
                if j < 6:
                    pcur = pn
                qcur = qn

            ptW, ptrW = bpool.alloc()
            for i in range(4):
                kb.op("pe", lambda e, ptW=ptW, i=i, rc=rcur:
                      e.transpose(ptW[base:base + 64, i * 128:(i + 1) * 128],
                                  Rb[rc][:, i * 128:i * 128 + 64],
                                  identbf[:, :]),
                      reads=[t_R[rcur], t_consts], writes=[ptrW], tag="dWt")
            kb.op("ac", lambda e, ptW=ptW:
                  e.copy(wfmw[base:base + 64, 0:512], ptW[base:base + 64, 0:512]),
                  reads=[ptrW], writes=[t_wfmw], tag="dWe")
            yield

            ptWS, ptrWS = fpool.alloc()
            for i in range(4):
                kb.op("pe", lambda e, ptWS=ptWS, i=i, pi=pis[i]:
                      e.matmul(ptWS[:, i * 64:(i + 1) * 64],
                               wfmw[base:base + 64, i * 128:(i + 1) * 128],
                               Sbf[base:base + 64, pi * 64:(pi + 1) * 64],
                               start=True, stop=True),
                      reads=[t_wfmw, t_Sbf[pis[i] * 2 + par]],
                      writes=[ptrWS], tag="dWS")
            kb.op("dv", lambda e, ptWS=ptWS, rc=rcur:
                  e.tensor_sub(
                      uprs[:, 0:256].rearrange("p (c f) -> p c f", c=4),
                      Rb[rc][:, :].rearrange("p (c f) -> p c f",
                                             c=4)[:, :, 64:128],
                      ptWS[:, 0:256].rearrange("p (c f) -> p c f", c=4)),
                  reads=[ptrWS, t_R[rcur]], writes=[t_upr], hazard=True,
                  tag="dUs")
            for i in range(4):
                kb.op("dv", lambda e, i=i, h=hs[i]:
                      e.tensor_scalar(uprs[:, i * 64:(i + 1) * 64],
                                      uprs[:, i * 64:(i + 1) * 64],
                                      btm[:, g, h:h + 1], None, ALU.mult),
                      reads=[t_btm[g], t_upr], writes=[t_upr],
                      hazard=True, tag="dUb")
            yield

            ptT, ptrT = fpool.alloc()
            for i in range(4):
                kb.op("pe", lambda e, ptT=ptT, i=i, ic=ics[i]:
                      e.matmul(ptT[:, i * 128:(i + 1) * 128],
                               knsl(i),
                               qfm[base:base + 64, ic, g * 128:(g + 1) * 128],
                               start=True, stop=True),
                      reads=[t_knfm, t_qfm[ics[i]]], writes=[ptrT], tag="dT")
            for i in range(4):
                kb.op("dv", lambda e, ptT=ptT, i=i:
                      e.tensor_mul(ttrs[:, i * 128:(i + 1) * 128],
                                   ptT[:, i * 128:(i + 1) * 128],
                                   triis[:, 0:128]),
                      reads=[ptrT, t_consts], writes=[t_ttr], tag="dTm")
            yield

            ptO, ptrO = fpool.alloc()
            for i in range(4):
                ic = ics[i]
                kb.op("pe", lambda e, ptO=ptO, i=i, pi=pis[i], ic=ic:
                      e.matmul(ptO[base:base + 64, i * 128:(i + 1) * 128],
                               Sbf[base:base + 64, pi * 64:(pi + 1) * 64],
                               qfm[base:base + 64, ic, g * 128:(g + 1) * 128],
                               start=True, stop=False),
                      reads=[t_Sbf[pis[i] * 2 + par], t_qfm[ic]],
                      writes=[ptrO], tag="dO1")
                kb.op("pe", lambda e, ptO=ptO, i=i:
                      e.matmul(ptO[base:base + 64, i * 128:(i + 1) * 128],
                               uprs[:, i * 64:(i + 1) * 64],
                               ttrs[:, i * 128:(i + 1) * 128],
                               start=False, stop=True),
                      reads=[t_upr, t_ttr], writes=[ptrO], tag="dO2")
            for i in range(4):
                ic = ics[i]
                kb.op("ac", lambda e, ptO=ptO, i=i, ic=ic:
                      e.copy(qfm[base:base + 64, ic, g * 128:(g + 1) * 128],
                             ptO[base:base + 64, i * 128:(i + 1) * 128]),
                      reads=[ptrO], writes=[t_qfm[ic]], tag="dOe")
            yield

            ptD, ptrD = fpool.alloc()
            for i in range(4):
                h = hs[i]
                kb.op("pe", lambda e, ptD=ptD, i=i, h=h:
                      e.matmul(ptD[base:base + 64, i * 64:(i + 1) * 64],
                               ktm[:, g, h * 64:(h + 1) * 64],
                               uprs[:, i * 64:(i + 1) * 64],
                               start=True, stop=True),
                      reads=[t_ktm[g], t_upr], writes=[ptrD], tag="dDS")
            s0 = pis[0] * 64
            kb.op("dv", lambda e, ptD=ptD, s0=s0:
                  e.tensor_add(S32[base:base + 64, s0:s0 + 256],
                               S32[base:base + 64, s0:s0 + 256],
                               ptD[base:base + 64, 0:256]),
                  reads=[ptrD] + [t_S32[pi * 2 + par] for pi in pis],
                  writes=[t_S32[pi * 2 + par] for pi in pis], tag="dDe")
            kb.op("ac", lambda e, s0=s0:
                  e.copy(Sbf[base:base + 64, s0:s0 + 256],
                         S32[base:base + 64, s0:s0 + 256]),
                  reads=[t_S32[pi * 2 + par] for pi in pis],
                  writes=[t_Sbf[pi * 2 + par] for pi in pis], tag="dDc")


        for ft in range(N_FFN):
            for at in range(TILES_PER_FFN):
                tt = ft * TILES_PER_FFN + at
                W0 = tt * C
                attn_fence = kb.fence() if at == 0 else []

                for b in range(B_LOC):
                    gg = at * 4 + b
                    dma("sy", x1[:, gg, :], x_d[b, W0:W0 + C, :], f"xl{tt}",
                        writes=[t_x1[gg]])

                if not SKIP_ATTN:
                    wload(wbufA, t_wbufA, wq_d, f"wq{tt}", extra=attn_fence, sidx=0, tt=tt)
                    wload(wbufB, t_wbufB, wk_d, f"wk{tt}", extra=attn_fence, sidx=1, tt=tt)
                    dma("gp", wbetabf[:, :, :],
                        wbeta_d.rearrange("(dc p) h -> p dc h", p=128),
                        f"wb{tt}", writes=[t_wbetabf], deps=attn_fence)

                    if attn_fence:
                        kb.op("dv", lambda e: e.engine_nop(), deps=attn_fence,
                              tag="fdv")

                    for b in range(B_LOC):
                        emit_ln(at * 4 + b, xnfm, [t_xnfm], b * 128, f"l1_{tt}{b}")

                    for ic in range(8):
                        pt, ptr = fpool.alloc()
                        for dc in range(8):
                            kb.op("pe", lambda e, pt=pt, ic=ic, dc=dc:
                                  e.matmul(pt[:, 0:512],
                                           wbufA[:, dc, ic * 128:(ic + 1) * 128],
                                           xnfm[:, dc, :],
                                           start=(dc == 0), stop=(dc == 7)),
                                  reads=[t_wbufA, t_xnfm], writes=[ptr], tag="qmm")
                        kb.op("ac", lambda e, pt=pt, ic=ic:
                              e.copy(qfm[:, ic, :], pt[:, 0:512]),
                              reads=[ptr], writes=[t_qfm[ic]], tag="qev")

                    def emit_knorm(g):
                        rn = rnorm[:, g * 16:(g + 1) * 16]
                        trn = t_rnorm[g]
                        for half in range(2):
                            kb.op("ac", lambda e, g=g, half=half:
                                  e.activation(attsc[:, 0:512],
                                               ktm[:, g,
                                                   half * 512:(half + 1) * 512],
                                               AF.Square),
                                  reads=[t_ktm[g]], writes=[t_attsc],
                                  tag="ksq")
                            kb.op("dv", lambda e, rn=rn, half=half:
                                  e.tensor_reduce(rn[:, half * 8:half * 8 + 8],
                                                  attsc[:, 0:512].rearrange(
                                                      "p (h d) -> p h d", h=8),
                                                  AX.X, ALU.add),
                                  reads=[t_attsc], writes=[trn], hazard=True,
                                  tag="krd")
                        kb.op("dv", lambda e, rn=rn:
                              e.tensor_scalar(rn[:, 0:16], rn[:, 0:16],
                                              1e-24, None, ALU.max),
                              reads=[trn], writes=[trn], hazard=True,
                              tag="kcl")
                        kb.op("ac", lambda e, rn=rn:
                              e.activation(rn[:, 0:16], rn[:, 0:16], AF.Ln),
                              reads=[trn], writes=[trn], hazard=True, tag="kln")
                        kb.op("dv", lambda e, rn=rn:
                              e.tensor_scalar(rn[:, 0:16], rn[:, 0:16],
                                              -0.5, None, ALU.mult),
                              reads=[trn], writes=[trn], hazard=True, tag="kmh")
                        kb.op("ac", lambda e, rn=rn:
                              e.activation(rn[:, 0:16], rn[:, 0:16], AF.Exp),
                              reads=[trn], writes=[trn], hazard=True, tag="kex")
                        for h in range(16):
                            kb.op("dv", lambda e, g=g, h=h, rn=rn:
                                  e.tensor_scalar(ktm[:, g, h * 64:(h + 1) * 64],
                                                  ktm[:, g, h * 64:(h + 1) * 64],
                                                  rn[:, h:h + 1], None,
                                                  ALU.mult),
                                  reads=[trn, t_ktm[g]], writes=[t_ktm[g]],
                                  tag="knm")

                    for g in range(4):
                        for half in range(2):
                            pt, ptr = fpool.alloc()
                            for dc in range(8):
                                kb.op("pe", lambda e, pt=pt, g=g, half=half, dc=dc:
                                      e.matmul(pt[:, 0:512],
                                               xnfm[:, dc, g * 128:(g + 1) * 128],
                                               wbufB[:, dc,
                                                     half * 512:(half + 1) * 512],
                                               start=(dc == 0), stop=(dc == 7)),
                                      reads=[t_wbufB, t_xnfm], writes=[ptr],
                                      tag="kmm")
                            kb.op("ac", lambda e, pt=pt, g=g, half=half:
                                  e.copy(ktm[:, g, half * 512:(half + 1) * 512],
                                         pt[:, 0:512]),
                                  reads=[ptr], writes=[t_ktm[g]], tag="kev")
                        if not SKIP_DN:
                            emit_knorm(g)

                    wload(wbufA, t_wbufA, wv_d, f"wv{tt}", sidx=2, tt=tt)
                    for g in range(4):
                        for half in range(2):
                            pt, ptr = fpool.alloc()
                            for dc in range(8):
                                kb.op("pe", lambda e, pt=pt, g=g, half=half, dc=dc:
                                      e.matmul(pt[:, 0:512],
                                               xnfm[:, dc, g * 128:(g + 1) * 128],
                                               wbufA[:, dc,
                                                     half * 512:(half + 1) * 512],
                                               start=(dc == 0), stop=(dc == 7)),
                                      reads=[t_wbufA, t_xnfm], writes=[ptr],
                                      tag="vmm")
                            kb.op("ac", lambda e, pt=pt, g=g, half=half:
                                  e.copy(vtm[:, g, half * 512:(half + 1) * 512],
                                         pt[:, 0:512]),
                                  reads=[ptr], writes=[t_vtm[g]], tag="vev")

                    wload(wbufB, t_wbufB, wout_d, f"wo{tt}", sidx=3, tt=tt)

                    for g in range(4):
                        pt, ptr = fpool.alloc()
                        for dc in range(8):
                            kb.op("pe", lambda e, pt=pt, g=g, dc=dc:
                                  e.matmul(pt[0:16, 0:128],
                                           wbetabf[:, dc, :],
                                           xnfm[:, dc, g * 128:(g + 1) * 128],
                                           start=(dc == 0), stop=(dc == 7)),
                                  reads=[t_wbetabf, t_xnfm], writes=[ptr],
                                  tag="bmm")
                        kb.op("ac", lambda e, pt=pt, g=g:
                              e.activation(bsig[:, g * 128:(g + 1) * 128],
                                           pt[0:16, 0:128], AF.Sigmoid,
                                           bias=bbetas[:, 0:1]),
                              reads=[ptr, t_cstf], writes=[t_bsig], tag="bsg")
                        pt2, ptr2 = fpool.alloc()
                        kb.op("pe", lambda e, pt2=pt2, g=g:
                              e.transpose(pt2[0:128, 0:16],
                                          bsig[:, g * 128:(g + 1) * 128],
                                          identf[0:16, 0:16]),
                              reads=[t_bsig, t_consts], writes=[ptr2], tag="btr")
                        kb.op("dv", lambda e, pt2=pt2, g=g:
                              e.tensor_copy(btm[:, g, :], pt2[0:128, 0:16]),
                              reads=[ptr2], writes=[t_btm[g]], hazard=True,
                          tag="btv")

                    wave_list = []
                    for g in range(4 if not SKIP_DN else 0):
                        for hb in range(2):
                            for par in range(2):
                                wave_list.append(
                                    (g, [par + 2 * (4 * hb + i)
                                         for i in range(4)]))
                    if wave_list:
                        gens = [emit_wave(*wave_list[0], tt, 0),
                                emit_wave(*wave_list[1], tt, 1)]
                        for _ in range(5):
                            next(gens[0])
                        nxt_w = 2
                        turn = 0
                        while gens[0] is not None or gens[1] is not None:
                            i = turn % 2
                            if gens[i] is not None:
                                try:
                                    next(gens[i])
                                except StopIteration:
                                    if nxt_w < len(wave_list):
                                        gens[i] = emit_wave(
                                            *wave_list[nxt_w], tt, i)
                                        nxt_w += 1
                                    else:
                                        gens[i] = None
                            turn += 1

                    for dc in range(8):
                        pt, ptr = fpool.alloc()
                        for ic in range(8):
                            kb.op("pe", lambda e, pt=pt, dc=dc, ic=ic:
                                  e.matmul(pt[:, 0:512],
                                           wbufB[:, ic, dc * 128:(dc + 1) * 128],
                                           qfm[:, ic, :],
                                           start=(ic == 0), stop=(ic == 7)),
                                  reads=[t_wbufB, t_qfm[ic]], writes=[ptr],
                                  tag="womm")
                        kb.op("dv", lambda e, pt=pt:
                              e.tensor_copy(attsc[:, 0:512], pt[:, 0:512]),
                              reads=[ptr], writes=[t_attsc], tag="woev")
                        pt2, ptr2 = fpool.alloc()
                        for j in range(4):
                            kb.op("pe", lambda e, pt2=pt2, j=j:
                                  e.transpose(pt2[:, j * 128:(j + 1) * 128],
                                              attsc[:, j * 128:(j + 1) * 128],
                                              identf[:, :]),
                                  reads=[t_attsc, t_consts], writes=[ptr2],
                                  tag="wotr")
                        kb.op("dv", lambda e, pt2=pt2, dc=dc, at=at:
                              e.tensor_tensor(
                                  x1[:, at * 4:(at + 1) * 4,
                                     dc * 128:(dc + 1) * 128],
                                  x1[:, at * 4:(at + 1) * 4,
                                     dc * 128:(dc + 1) * 128],
                                  pt2[:, 0:512].rearrange("p (j t) -> p j t", j=4),
                                  ALU.add),
                              reads=[ptr2] + [t_x1[at * 4 + j] for j in range(4)],
                              writes=[t_x1[at * 4 + j] for j in range(4)],
                              tag="wors")

                for b in range(B_LOC):
                    ggf = at * 4 + b
                    emit_ln(ggf, xn2, t_xn2, ggf * 128, f"l2_{tt}{b}")
                    pt, ptr = fpool.alloc()
                    for dc in range(8):
                        kb.op("pe", lambda e, pt=pt, dc=dc, ggf=ggf:
                              e.matmul(pt[0:128, 0:8],
                                       xn2[:, dc, ggf * 128:(ggf + 1) * 128],
                                       cstb[:, 1160 + dc * 8:1168 + dc * 8],
                                       start=(dc == 0), stop=(dc == 7)),
                              reads=[t_xn2[dc // 4], t_cstb], writes=[ptr],
                              tag="gmm")
                    kb.op("dv", lambda e, pt=pt:
                          e.tensor_add(gcols[:, 8:16], pt[0:128, 0:8],
                                       cols[:, 8:16]),
                          reads=[ptr, t_cols], writes=[t_gcols],
                          hazard=True, tag="glg")
                    kb.op("dv", lambda e:
                          e.reduce_max(gcols[:, 0:1], gcols[:, 8:16], AX.X,
                                       negate=True),
                          reads=[t_gcols], writes=[t_gcols], hazard=True,
                          tag="gnm")
                    kb.op("ac", lambda e:
                          e.activation(gp_t[:, :], gcols[:, 8:16], AF.Exp,
                                       bias=gcols[:, 0:1]),
                          reads=[t_gcols], writes=[t_gp], tag="gex")
                    kb.op("dv", lambda e:
                          e.tensor_scalar(gind[:, :], gp_t[:, :], 1.0, None,
                                          ALU.is_equal),
                          reads=[t_gp], writes=[t_gind], hazard=True, tag="geq")
                    kb.op("dv", lambda e:
                          e.tensor_sub(gind[:, :], gp_t[:, :], gind[:, :]),
                          reads=[t_gp, t_gind], writes=[t_gind], hazard=True, tag="gsb")
                    kb.op("dv", lambda e:
                          e.reduce_max(gcols[:, 1:2], gind[:, :], AX.X),
                          reads=[t_gind], writes=[t_gcols], hazard=True,
                          tag="gm2")
                    kb.op("dv", lambda e:
                          e.tensor_scalar(gcols[:, 2:3], gcols[:, 1:2], 1.0,
                                          None, ALU.add),
                          reads=[t_gcols], writes=[t_gcols], hazard=True,
                          tag="gdn")
                    kb.op("dv", lambda e:
                          e.reciprocal(gcols[:, 3:4], gcols[:, 2:3]),
                          reads=[t_gcols], writes=[t_gcols], hazard=True,
                          tag="grc")
                    kb.op("dv", lambda e:
                          e.tensor_scalar(gind[:, :], gp_t[:, :],
                                          gcols[:, 1:2], None, ALU.is_ge),
                          reads=[t_gp, t_gcols], writes=[t_gind], hazard=True, tag="gge")
                    kb.op("dv", lambda e, ggf=ggf:
                          e.scalar_tensor_tensor(
                              wcols[:, ggf * 8:(ggf + 1) * 8],
                              gp_t[:, :], gcols[:, 3:4], gind[:, :],
                              ALU.mult, ALU.mult),
                          reads=[t_gp, t_gcols, t_gind], writes=[t_wcols],
                          tag="gw")
                    pt3, ptr3 = fpool.alloc()
                    kb.op("pe", lambda e, pt3=pt3, ggf=ggf:
                          e.transpose(pt3[0:8, 0:128],
                                      wcols[:, ggf * 8:(ggf + 1) * 8],
                                      identf[:, :]),
                          reads=[t_wcols, t_consts], writes=[ptr3], tag="gwt")
                    kb.op("ac", lambda e, pt3=pt3, ggf=ggf:
                          e.copy(wfm[:, ggf * 128:(ggf + 1) * 128],
                                 pt3[0:8, 0:128]),
                          reads=[ptr3], writes=[t_wfm], tag="gwf")

            ffn_fence = kb.fence()
            for gg in range(NG):
                for dh in range(2):
                    pt, ptr = fpool.alloc()
                    kb.op("pe", lambda e, pt=pt, gg=gg, dh=dh:
                          e.matmul(pt[:, 0:512],
                                   wfm[:, gg * 128:(gg + 1) * 128],
                                   b2bf[:, dh * 512:(dh + 1) * 512],
                                   start=True, stop=True),
                          reads=[t_wfm, t_cstb], writes=[ptr], tag="b2m")
                    kb.op("dv", lambda e, pt=pt, gg=gg, dh=dh:
                          e.tensor_tensor(x1[:, gg, dh * 512:(dh + 1) * 512],
                                          x1[:, gg, dh * 512:(dh + 1) * 512],
                                          pt[:, 0:512], ALU.add),
                          reads=[ptr, t_x1[gg]], writes=[t_x1[gg]], tag="b2e")

            for e_i in range(NE if not SKIP_FFN else 0):
                for fh in range(2):
                    first = (e_i == 0 and fh == 0)
                    w1r = w1_d[e_i].rearrange("(dc p) f -> p dc f", p=128)
                    f0 = fh * 2048
                    if ft == 0:
                        for dc4 in range(2):
                            for s2 in range(2):
                                dma("gp",
                                    w1h[:, dc4 * 4:(dc4 + 1) * 4,
                                        s2 * 1024:(s2 + 1) * 1024],
                                    w1r[:, dc4 * 4:(dc4 + 1) * 4,
                                        f0 + s2 * 1024:f0 + (s2 + 1) * 1024],
                                    "w1", writes=[t_w1h],
                                    deps=(ffn_fence if first else []))
                        dma("sy", w1s_d[e_i, fh], w1h[:, :, :], "w1st",
                            reads=[t_w1h])
                    else:
                        dma("sy", w1h[:, :, :], w1s_d[e_i, fh], "w1ld",
                            writes=[t_w1h],
                            deps=(ffn_fence if first else []))
                    w2r = w2_d[e_i].rearrange("(fc p) d -> p fc d", p=128)
                    fc0 = fh * 16

                    for tc in range(4):
                        tok0 = tc * 512
                        for fc in range(16):
                            pt, ptr = fpool.alloc(subset=(4, 5))
                            for dc in range(8):
                                kb.op("pe", lambda e, pt=pt, fc=fc, dc=dc,
                                      tok0=tok0:
                                      e.matmul(pt[:, 0:512],
                                               w1h[:, dc,
                                                   fc * 128:(fc + 1) * 128],
                                               xn2[:, dc, tok0:tok0 + 512],
                                               start=(dc == 0), stop=(dc == 7)),
                                      reads=[t_w1h, t_xn2[dc // 4]],
                                      writes=[ptr], tag="hmm")
                            fcg = fh * 16 + fc
                            kb.op("ac", lambda e, pt=pt, fc=fc, fcg=fcg,
                                  e_i=e_i:
                                  e.activation(hg[:, fc, :], pt[:, 0:512],
                                               AF.Gelu,
                                               bias=cstf[:, 1 + fcg * 8 + e_i:
                                                         2 + fcg * 8 + e_i]),
                                  reads=[ptr, t_cstf], writes=[t_hg],
                                  tag="gel")
                        if tc == 0:
                            if ft == 0:
                                for q in range(4):
                                    dma("gp", w2h[:, q * 4:(q + 1) * 4, :],
                                        w2r[:, fc0 + q * 4:fc0 + (q + 1) * 4, :],
                                        "w2", writes=[t_w2h],
                                        deps=(ffn_fence if first else []))
                                dma("sy", w2s_d[e_i, fh], w2h[:, :, :], "w2st",
                                    reads=[t_w2h])
                            else:
                                dma("sy", w2h[:, :, :], w2s_d[e_i, fh], "w2ld",
                                    writes=[t_w2h],
                                    deps=(ffn_fence if first else []))
                        for gl in range(4):
                            gg = tc * 4 + gl
                            sub = (0, 1) if gl % 2 == 0 else (2, 3)
                            ptA, ptrA = fpool.alloc(subset=sub)
                            ptB, ptrB = fpool.alloc(subset=sub)
                            for fc in range(16):
                                kb.op("pe", lambda e, ptA=ptA, fc=fc, gl=gl:
                                      e.matmul(ptA[:, 0:512],
                                               hg[:, fc,
                                                  gl * 128:(gl + 1) * 128],
                                               w2h[:, fc, 0:512],
                                               start=(fc == 0),
                                               stop=(fc == 15)),
                                      reads=[t_hg, t_w2h], writes=[ptrA],
                                      tag="ymA")
                            for fc in range(16):
                                kb.op("pe", lambda e, ptB=ptB, fc=fc, gl=gl:
                                      e.matmul(ptB[:, 0:512],
                                               hg[:, fc,
                                                  gl * 128:(gl + 1) * 128],
                                               w2h[:, fc, 512:1024],
                                               start=(fc == 0),
                                               stop=(fc == 15)),
                                      reads=[t_hg, t_w2h], writes=[ptrB],
                                      tag="ymB")
                            kb.op("dv", lambda e, ptA=ptA, gg=gg, e_i=e_i:
                                  e.scalar_tensor_tensor(
                                      x1[:, gg, 0:512], ptA[:, 0:512],
                                      wcols[:, gg * 8 + e_i:gg * 8 + e_i + 1],
                                      x1[:, gg, 0:512], ALU.mult, ALU.add),
                                  reads=[ptrA, t_wcols, t_x1[gg]],
                                  writes=[t_x1[gg]], tag="yeA")
                            kb.op("dv", lambda e, ptB=ptB, gg=gg, e_i=e_i:
                                  e.scalar_tensor_tensor(
                                      x1[:, gg, 512:1024], ptB[:, 0:512],
                                      wcols[:, gg * 8 + e_i:gg * 8 + e_i + 1],
                                      x1[:, gg, 512:1024], ALU.mult, ALU.add),
                                  reads=[ptrB, t_wcols, t_x1[gg]],
                                  writes=[t_x1[gg]], tag="yeB")

            for gg in range(NG):
                at_, b = gg // 4, gg % 4
                W0 = (ft * TILES_PER_FFN + at_) * C
                dma("gp", out_d[b, W0:W0 + C, :], x1[:, gg, :], f"st{ft}",
                    reads=[t_x1[gg]])

        sy_total = sum(16 for o in kb.ops if o.eng == "sy" and o.dma)
        gp_total = sum(16 for o in kb.ops if o.eng == "gp" and o.dma)
        kb.op("sy", lambda e, v=sy_total: e.wait_ge(sems["sy"], v), tag="fin")
        kb.op("sy", lambda e, v=gp_total: e.wait_ge(sems["gp"], v), tag="fin2")

        kb.finalize_and_emit(sems)

    return nc


_NC_CACHE = {}


def kernel(L0_attn_x, L0_attn_Wq, L0_attn_Wk, L0_attn_Wv, L0_attn_Wbeta,
           L0_attn_b_beta, L0_attn_Wout, L0_ffn_Wg, L0_ffn_bg, L0_ffn_W1,
           L0_ffn_b1, L0_ffn_W2, L0_ffn_b2):
    x = np.ascontiguousarray(np.asarray(L0_attn_x, dtype=np.float32))
    n_cores = 8
    if "nc" not in _NC_CACHE:
        _NC_CACHE["nc"] = build_nc()
    nc = _NC_CACHE["nc"]

    b1 = np.asarray(L0_ffn_b1, dtype=np.float32)
    cst_f = np.zeros((128, 257), dtype=np.float32)
    cst_f[0:H, 0] = np.asarray(L0_attn_b_beta, dtype=np.float32).reshape(H)
    cst_f[:, 1:257] = b1.reshape(NE, 32, 128).transpose(2, 1, 0).reshape(128, 256)
    cst_b = np.zeros((128, 1224), dtype=np.float32)
    cst_b[0:NE, 0:1024] = np.asarray(L0_ffn_b2, dtype=np.float32)
    cst_b[0, 1024:1032] = np.asarray(L0_ffn_bg, dtype=np.float32).reshape(NE)
    cst_b[0, 1032:1160] = 1.0
    wg = np.asarray(L0_ffn_Wg, dtype=np.float32)
    cst_b[:, 1160:1224] = wg.reshape(8, 128, NE).transpose(1, 0, 2).reshape(128, 64)

    common = {
        "wq": np.ascontiguousarray(L0_attn_Wq, dtype=np.float32),
        "wk": np.ascontiguousarray(L0_attn_Wk, dtype=np.float32),
        "wv": np.ascontiguousarray(L0_attn_Wv, dtype=np.float32),
        "wbeta": np.ascontiguousarray(L0_attn_Wbeta, dtype=np.float32),
        "wout": np.ascontiguousarray(L0_attn_Wout, dtype=np.float32),
        "w1": np.ascontiguousarray(L0_ffn_W1, dtype=np.float32),
        "w2": np.ascontiguousarray(L0_ffn_W2, dtype=np.float32),
        "cst_f": cst_f,
        "cst_b": cst_b,
    }
    in_maps = []
    for c in range(n_cores):
        m = dict(common)
        m["x"] = np.ascontiguousarray(x[c * B_LOC:(c + 1) * B_LOC])
        in_maps.append(m)

    res = run_bass_kernel_spmd(nc, in_maps, core_ids=list(range(n_cores)))
    out = np.concatenate([np.asarray(r["out"]) for r in res.results], axis=0)
    return out.astype(np.float32)


# version 11
# speedup vs baseline: 1.0630x; 1.0630x over previous
import numpy as np
import ml_dtypes
import concourse.bass as bass
import concourse.mybir as mybir
from concourse.bass_utils import run_bass_kernel_spmd

F32 = mybir.dt.float32
BF16 = mybir.dt.bfloat16
AF = mybir.ActivationFunctionType
ALU = mybir.AluOpType
AX = mybir.AxisListType

B_LOC = 4
T = 2048
D = 1024
H = 16
DH = 64
C = 128
N_TILE = 16
TILES_PER_FFN = 4
N_FFN = N_TILE // TILES_PER_FFN
NE = 8
DFF = 4096
NG = 16
SKIP_ATTN = False
SKIP_FFN = False
SKIP_DN = False

ENGS = ("sy", "gp", "pe", "dv", "ac")


class Tr:
    __slots__ = ("w", "r", "name")

    def __init__(self, name=""):
        self.w = {}
        self.r = {}
        self.name = name


class Op:
    __slots__ = ("eng", "fn", "deps", "hazard", "dma", "inc", "seq",
                 "sem_after", "waits", "tag")

    def __init__(self, eng, fn, hazard, dma, tag):
        self.eng = eng
        self.fn = fn
        self.deps = []
        self.hazard = hazard
        self.dma = dma
        self.inc = False
        self.seq = -1
        self.sem_after = 0
        self.waits = None
        self.tag = tag


class KB:
    def __init__(self, nc):
        self.nc = nc
        self.ops = []

    def op(self, eng, fn, reads=(), writes=(), deps=(), hazard=False,
           dma=False, tag=""):
        o = Op(eng, fn, hazard, dma, tag)
        dd = list(deps)
        for tr in reads:
            dd.extend(tr.w.values())
        for tr in writes:
            dd.extend(tr.w.values())
            dd.extend(tr.r.values())
        o.deps = dd
        for tr in reads:
            tr.r[eng] = o
        for tr in writes:
            tr.w = {eng: o}
            tr.r = {}
        self.ops.append(o)
        return o

    def fence(self, engs=("pe", "dv", "ac")):
        last = {}
        for o in self.ops:
            if o.eng in engs:
                last[o.eng] = o
        return list(last.values())

    def finalize_and_emit(self, sems):
        for o in self.ops:
            for d in o.deps:
                if d is o:
                    continue
                if d.eng == o.eng and not d.dma and not d.hazard:
                    continue
                d.inc = True
        cnt = {e: 0 for e in ENGS}
        for o in self.ops:
            if o.dma:
                o.inc = True
            if o.inc:
                cnt[o.eng] += 16 if o.dma else 1
                o.sem_after = cnt[o.eng]
        prev_dma = {e: None for e in ENGS}
        prev_dma_at = []
        for o in self.ops:
            prev_dma_at.append(dict(prev_dma))
            if o.dma:
                prev_dma[o.eng] = o
        for idx, o in enumerate(self.ops):
            w = {}
            for d in o.deps:
                if d is o:
                    continue
                if d.eng == o.eng and not d.dma and not d.hazard:
                    continue
                if d.dma:
                    lat = prev_dma_at[idx][d.eng]
                    v = lat.sem_after if lat is not None else d.sem_after
                    assert v >= d.sem_after
                else:
                    v = d.sem_after
                assert v > 0, f"dep without inc: {d.tag} <- {o.tag}"
                if w.get(d.eng, 0) < v:
                    w[d.eng] = v
            o.waits = w
        self.totals = cnt

        ops_by_eng = {e: [o for o in self.ops if o.eng == e] for e in ENGS}

        ptr = {e: 0 for e in ENGS}
        sem = {e: 0 for e in ENGS}
        n_done = 0
        n_total = len(self.ops)
        while n_done < n_total:
            progressed = False
            for e in ENGS:
                while ptr[e] < len(ops_by_eng[e]):
                    o = ops_by_eng[e][ptr[e]]
                    if any(sem[se] < v for se, v in o.waits.items()):
                        break
                    if o.inc:
                        sem[e] = o.sem_after
                    ptr[e] += 1
                    n_done += 1
                    progressed = True
            if not progressed:
                stuck = {e: (ops_by_eng[e][ptr[e]].tag, ops_by_eng[e][ptr[e]].waits)
                         for e in ENGS if ptr[e] < len(ops_by_eng[e])}
                raise RuntimeError(f"deadlock in plan: {stuck}, sems={sem}")

        with self.nc.Block() as blk:
            def make_body(ename):
                def body(eng):
                    hw = {e: 0 for e in ENGS}
                    for o in ops_by_eng[ename]:
                        for se, val in sorted(o.waits.items()):
                            if val > hw[se]:
                                eng.wait_ge(sems[se], val)
                                hw[se] = val
                        inst = o.fn(eng)
                        if o.inc:
                            inst.then_inc(sems[o.eng], 16 if o.dma else 1)
                return body

            blk.sync(make_body("sy"))
            blk.gpsimd(make_body("gp"))
            blk.tensor(make_body("pe"))
            blk.vector(make_body("dv"))
            blk.scalar(make_body("ac"))


class PsumPool:
    def __init__(self, tensors, names):
        self.slots = [{"t": t, "tr": Tr(n), "stamp": i}
                      for i, (t, n) in enumerate(zip(tensors, names))]
        self.clock = len(tensors)

    def alloc(self, subset=None):
        cand = self.slots if subset is None else [self.slots[i] for i in subset]
        s = min(cand, key=lambda q: q["stamp"])
        s["stamp"] = self.clock
        self.clock += 1
        return s["t"], s["tr"]


class Arena:
    def __init__(self, nc, total_bytes, ctx):
        self.nc = nc
        self.guard = ctx.enter_context(
            nc.sbuf_tensor("arena_guard", [128, total_bytes], mybir.dt.uint8))
        self.base = nc.lookup_mloc(self.guard).addr
        assert self.base % 32 == 0, self.base

    def at(self, name, offset, shape, dtype):
        return self.nc.alloc_sbuf_tensor_at(name, list(shape), dtype,
                                            offset=self.base + offset)


def build_nc():
    nc = bass.Bass("TRN2", target_bir_lowering=False)

    x_d = nc.dram_tensor("x", [B_LOC, T, D], F32, kind="ExternalInput").ap()
    out_d = nc.dram_tensor("out", [B_LOC, T, D], F32, kind="ExternalOutput").ap()
    wq_d = nc.dram_tensor("wq", [D, D], F32, kind="ExternalInput").ap()
    wk_d = nc.dram_tensor("wk", [D, D], F32, kind="ExternalInput").ap()
    wv_d = nc.dram_tensor("wv", [D, D], F32, kind="ExternalInput").ap()
    wbeta_d = nc.dram_tensor("wbeta", [D, H], F32, kind="ExternalInput").ap()
    wout_d = nc.dram_tensor("wout", [D, D], F32, kind="ExternalInput").ap()
    w1_d = nc.dram_tensor("w1", [NE, D, DFF], F32, kind="ExternalInput").ap()
    w2_d = nc.dram_tensor("w2", [NE, DFF, D], F32, kind="ExternalInput").ap()
    cstf_d = nc.dram_tensor("cst_f", [128, 257], F32, kind="ExternalInput").ap()
    cstb_d = nc.dram_tensor("cst_b", [128, 1224], F32, kind="ExternalInput").ap()
    w1s_d = nc.dram_tensor("w1stage", [NE, 2, 128, 16384], BF16,
                           kind="Internal").ap()
    w2s_d = nc.dram_tensor("w2stage", [NE, 2, 128, 16384], BF16,
                           kind="Internal").ap()
    was_d = nc.dram_tensor("wastage", [4, 128, 8192], BF16,
                           kind="Internal").ap()

    ident_bf = nc.inline_tensor(np.eye(128, dtype=ml_dtypes.bfloat16), "identbf").ap()
    ident_f = nc.inline_tensor(np.eye(128, dtype=np.float32), "identf").ap()
    triu1 = np.triu(np.ones((128, 128), dtype=np.float32), 1)
    trii = np.triu(np.ones((128, 128), dtype=np.float32), 0)
    triu1_d = nc.inline_tensor(triu1.astype(ml_dtypes.bfloat16), "triu1c").ap()
    trii_d = nc.inline_tensor(trii.astype(ml_dtypes.bfloat16), "triic").ap()

    from contextlib import ExitStack
    es = ExitStack()
    with es:
        off = 0

        def nxt(nbytes, align=32):
            nonlocal off
            off = (off + align - 1) // align * align
            o = off
            off += nbytes
            return o

        TOTAL = 212832
        ar = Arena(nc, TOTAL, es)
        A = ar.at

        o_identbf = nxt(256); o_identf = nxt(512)
        o_triu1 = nxt(256); o_trii = nxt(256)
        o_cstf = nxt(1028)
        o_cstb = nxt(2448)
        o_S = nxt(8192)
        o_Sbf = nxt(4096)
        o_wcols = nxt(512)
        o_wfm = nxt(4096)
        o_btm = nxt(256)
        o_bnb = nxt(48)
        o_cols = nxt(64)
        o_gcols = nxt(64)
        o_gp = nxt(32)
        o_gind = nxt(32)
        o_x1 = nxt(65536)
        o_xn2 = nxt(32768)
        o_arena = nxt(0)
        o_xntm = o_arena + 0
        o_xnfm = o_arena + 2048
        o_qfm = o_arena + 10240
        o_ktm = o_arena + 18432
        o_vtm = o_arena + 26624
        o_wbufA = o_arena + 34816
        o_wbufB = o_arena + 51200
        o_wbetabf = o_arena + 67584
        o_attsc = o_arena + 67840
        o_bsig = o_arena + 69888
        o_rnorm = o_arena + 71936
        aoff = o_arena + 72192
        o_knfm = []
        o_P = []
        o_Q = []
        o_R = []
        o_Ttr = []
        o_Upr = []
        o_wfmw = []
        for _ws in range(2):
            o_knfm.append(aoff); aoff += 1024
            for _k in range(2):
                o_P.append(aoff); aoff += 1024
                o_Q.append(aoff); aoff += 1024
                o_R.append(aoff); aoff += 1024
            o_Ttr.append(aoff); aoff += 1024
            o_Upr.append(aoff); aoff += 512
            o_wfmw.append(aoff); aoff += 1024
        o_hg = o_arena + 0
        o_w1h = o_arena + 16384
        o_w2h = o_arena + 49152
        assert max(aoff, o_arena + 81920) <= TOTAL, (aoff, TOTAL)

        identbf = A("identbf_s", o_identbf, [128, 128], BF16)
        identf = A("identf_s", o_identf, [128, 128], F32)
        triu1s = A("triu1_s", o_triu1, [128, 128], BF16)
        triis = A("trii_s", o_trii, [128, 128], BF16)
        cstf = A("cstf_s", o_cstf, [128, 257], F32)
        cstb = A("cstb_s", o_cstb, [128, 1224], BF16)
        bbetas = cstf[0:16, 0:1]
        b2bf = cstb[0:8, 0:1024]
        bgbf = cstb[0:1, 1024:1032]
        oness = cstb[0:1, 1032:1160]
        S32 = A("S32", o_S, [128, 2048], F32)
        Sbf = A("Sbf", o_Sbf, [128, 2048], BF16)
        wcols = A("wcols", o_wcols, [128, 128], F32)
        wfm = A("wfm", o_wfm, [8, 2048], BF16)
        bsig = A("bsig", o_bsig, [16, 512], F32)
        btm = A("btm", o_btm, [128, 4, 16], F32)
        bnb = A("bnb", o_bnb, [128, 12], F32)
        cols = A("colsx", o_cols, [128, 16], F32)
        gcols = A("gcols", o_gcols, [128, 16], F32)
        gp_t = A("gp_t", o_gp, [128, 8], F32)
        gind = A("gind", o_gind, [128, 8], F32)
        x1 = A("x1", o_x1, [128, 16, 1024], F32)
        xn2 = A("xn2", o_xn2, [128, 8, 2048], BF16)
        xntm = A("xntm", o_xntm, [128, 1024], BF16)
        xnfm = A("xnfm", o_xnfm, [128, 8, 512], BF16)
        qfm = A("qfm", o_qfm, [128, 8, 512], BF16)
        ktm = A("ktm", o_ktm, [128, 4, 1024], BF16)
        vtm = A("vtm", o_vtm, [128, 4, 1024], BF16)
        wbufA = A("wbufA", o_wbufA, [128, 8, 1024], BF16)
        wbufB = A("wbufB", o_wbufB, [128, 8, 1024], BF16)
        wbetabf = A("wbetabf", o_wbetabf, [128, 8, 16], BF16)
        attsc = A("attsc", o_attsc, [128, 512], F32)
        rnorm = A("rnorm", o_rnorm, [128, 64], F32)
        knfm_s = [A(f"knfm{w}", o_knfm[w], [128, 512], BF16) for w in range(2)]
        Pb_s = [[A(f"Pb{w}{k}", o_P[w * 2 + k], [128, 512], BF16)
                 for k in range(2)] for w in range(2)]
        Qb_s = [[A(f"Qb{w}{k}", o_Q[w * 2 + k], [128, 512], BF16)
                 for k in range(2)] for w in range(2)]
        Rb_s = [[A(f"Rb{w}{k}", o_R[w * 2 + k], [128, 512], BF16)
                 for k in range(2)] for w in range(2)]
        ttrs_s = [A(f"ttrs{w}", o_Ttr[w], [128, 512], BF16) for w in range(2)]
        uprs_s = [A(f"uprs{w}", o_Upr[w], [128, 256], BF16) for w in range(2)]
        wfmw_s = [A(f"wfmw{w}", o_wfmw[w], [128, 512], BF16) for w in range(2)]
        hg = A("hg", o_hg, [128, 16, 512], BF16)
        w1h = A("w1h", o_w1h, [128, 8, 2048], BF16)
        w2h = A("w2h", o_w2h, [128, 16, 1024], BF16)

        psf = [es.enter_context(nc.psum_tensor(f"psf{i}", [128, 512], F32))
               for i in range(6)]
        psb = [es.enter_context(nc.psum_tensor(f"psb{i}", [128, 1024], BF16))
               for i in range(2)]
        sems = {e: es.enter_context(nc.semaphore(f"sem_{e}")) for e in ENGS}

        kb = KB(nc)
        fpool = PsumPool(psf, [f"psf{i}" for i in range(6)])
        bpool = PsumPool(psb, [f"psb{i}" for i in range(2)])

        t_consts = Tr("consts")
        t_x1 = [Tr(f"x1_{g}") for g in range(NG)]
        t_xn2 = [Tr("xn2_lo"), Tr("xn2_hi")]
        t_xntm = Tr("xntm")
        t_xnfm = Tr("xnfm")
        t_qfm = [Tr(f"qfm{i}") for i in range(8)]
        t_ktm = [Tr(f"ktm{g}") for g in range(4)]
        t_vtm = [Tr(f"vtm{g}") for g in range(4)]
        t_wbufA = Tr("wbufA")
        t_wbufB = Tr("wbufB")
        t_wbetabf = Tr("wbetabf")
        t_bsig = Tr("bsig")
        t_btm = [Tr(f"btm{g}") for g in range(4)]
        t_bnb = Tr("bnb")
        t_cols = Tr("cols")
        t_gcols = Tr("gcols")
        t_gp = Tr("gp")
        t_gind = Tr("gind")
        t_S32 = [Tr(f"S32_{p}") for p in range(64)]
        t_Sbf = [Tr(f"Sbf_{p}") for p in range(64)]
        t_wcols = Tr("wcols")
        t_wfm = Tr("wfm")
        t_attsc = Tr("attsc")
        t_rnorm = [Tr(f"rnorm{g}") for g in range(4)]
        t_knfm_s = [Tr(f"knfm{w}") for w in range(2)]
        t_P_s = [[Tr(f"P{w}{k}") for k in range(2)] for w in range(2)]
        t_Q_s = [[Tr(f"Q{w}{k}") for k in range(2)] for w in range(2)]
        t_R_s = [[Tr(f"R{w}{k}") for k in range(2)] for w in range(2)]
        t_ttr_s = [Tr(f"ttr{w}") for w in range(2)]
        t_upr_s = [Tr(f"upr{w}") for w in range(2)]
        t_wfmw_s = [Tr(f"wfmw{w}") for w in range(2)]
        t_hg = Tr("hg")
        t_w1h = Tr("w1h")
        t_w2h = Tr("w2h")
        t_cstf = Tr("cstf")
        t_cstb = Tr("cstb")

        def dma(q, dst, src, tag, reads=(), writes=(), deps=(), mld=None):
            if mld is None:
                return kb.op(q, lambda e, d=dst, s=src:
                             e.dma_start(out=d, in_=s),
                             reads=reads, writes=writes, deps=deps, dma=True,
                             tag=tag)
            return kb.op(q, lambda e, d=dst, s=src, m=mld:
                         e.dma_start(out=d, in_=s, max_dma_last_dim=m),
                         reads=reads, writes=writes, deps=deps, dma=True,
                         tag=tag)

        dma("sy", identbf[:, :], ident_bf[:, :], "ldidb", writes=[t_consts])
        dma("sy", identf[:, :], ident_f[:, :], "ldidf", writes=[t_consts])
        dma("sy", triu1s[:, :], triu1_d[:, :], "ldtr1", writes=[t_consts])
        dma("sy", triis[:, :], trii_d[:, :], "ldtri", writes=[t_consts])
        dma("sy", cstf[:, :], cstf_d[:, :], "ldcf", writes=[t_cstf])
        dma("gp", cstb[:, :], cstb_d[:, :], "ldcb", writes=[t_cstb])

        ptbg, ptrbg = fpool.alloc()
        kb.op("pe", lambda e, ptbg=ptbg:
              e.matmul(ptbg[0:128, 0:8], oness[:, :], bgbf[:, :],
                       start=True, stop=True),
              reads=[t_cstb], writes=[ptrbg], tag="bgmm")
        kb.op("dv", lambda e, ptbg=ptbg:
              e.tensor_copy(cols[:, 8:16], ptbg[0:128, 0:8]),
              reads=[ptrbg], writes=[t_cols], hazard=True, tag="bgev")

        kb.op("dv", lambda e: e.memset(S32[:, :], 0.0),
              writes=list(t_S32), hazard=True, tag="S0")
        kb.op("ac", lambda e: e.copy(Sbf[:, :], S32[:, :]),
              reads=list(t_S32), writes=list(t_Sbf), tag="S0bf")

        def emit_ln(gg, dst_fm, dst_trs, dst_off, tag):
            xsl = x1[:, gg, :]
            src_tr = t_x1[gg]
            kb.op("dv", lambda e: e.bn_stats(bnb[:, 0:6], xsl[:, 0:512]),
                  reads=[src_tr], writes=[t_bnb], hazard=True, tag=f"{tag}b0")
            kb.op("dv", lambda e: e.bn_stats(bnb[:, 6:12], xsl[:, 512:1024]),
                  reads=[src_tr], writes=[t_bnb], hazard=True, tag=f"{tag}b1")
            kb.op("dv", lambda e: e.bn_aggr(cols[:, 1:3], bnb[:, 0:12]),
                  reads=[t_bnb], writes=[t_cols], hazard=True, tag=f"{tag}ba")
            kb.op("dv", lambda e: e.tensor_scalar(cols[:, 3:4], cols[:, 2:3],
                                                  1e-5, None, ALU.add),
                  reads=[t_cols], writes=[t_cols], hazard=True, tag=f"{tag}ve")
            kb.op("ac", lambda e: e.activation(cols[:, 4:5], cols[:, 3:4], AF.Ln),
                  reads=[t_cols], writes=[t_cols], hazard=True, tag=f"{tag}ln")
            kb.op("dv", lambda e: e.tensor_scalar(cols[:, 5:6], cols[:, 4:5],
                                                  -0.5, None, ALU.mult),
                  reads=[t_cols], writes=[t_cols], hazard=True, tag=f"{tag}mh")
            kb.op("ac", lambda e: e.activation(cols[:, 6:7], cols[:, 5:6], AF.Exp),
                  reads=[t_cols], writes=[t_cols], hazard=True, tag=f"{tag}ex")
            kb.op("dv", lambda e: e.tensor_scalar(xntm[:, :], xsl, cols[:, 1:2],
                                                  cols[:, 6:7], ALU.subtract,
                                                  ALU.mult),
                  reads=[src_tr, t_cols], writes=[t_xntm], tag=f"{tag}ap")
            for half in range(2):
                pt, ptr = bpool.alloc()
                for k in range(4):
                    dc = half * 4 + k
                    kb.op("pe", lambda e, pt=pt, dc=dc, k=k:
                          e.transpose(pt[:, k * 128:(k + 1) * 128],
                                      xntm[:, dc * 128:(dc + 1) * 128],
                                      identbf[:, :]),
                          reads=[t_xntm, t_consts], writes=[ptr], tag=f"{tag}tp")
                kb.op("ac", lambda e, pt=pt, half=half:
                      e.copy(dst_fm[:, half * 4:(half + 1) * 4,
                                    dst_off:dst_off + 128],
                             pt[:, 0:512].rearrange("p (k t) -> p k t", k=4)),
                      reads=[ptr], writes=dst_trs, tag=f"{tag}te")

        def wload(dst, tr, src2d, tag, extra=(), sidx=None, tt=None):
            if sidx is None or tt == 0:
                srcr = src2d.rearrange("(dc p) j -> p dc j", p=128)
                dma("gp", dst[:, 0:4, :], srcr[:, 0:4, :], tag + "a",
                    writes=[tr], deps=extra)
                dma("gp", dst[:, 4:8, :], srcr[:, 4:8, :], tag + "b",
                    writes=[tr], deps=extra)
                if sidx is not None:
                    dma("sy", was_d[sidx], dst[:, :, :], tag + "s", reads=[tr])
            else:
                dma("sy", dst[:, :, :], was_d[sidx], tag + "l",
                    writes=[tr], deps=extra)

        def emit_wave(g, hs, tt, ws):
            knfm = knfm_s[ws]
            Pb = Pb_s[ws]
            Qb = Qb_s[ws]
            Rb = Rb_s[ws]
            ttrs = ttrs_s[ws]
            uprs = uprs_s[ws]
            wfmw = wfmw_s[ws]
            t_knfm = t_knfm_s[ws]
            t_P = t_P_s[ws]
            t_Q = t_Q_s[ws]
            t_R = t_R_s[ws]
            t_ttr = t_ttr_s[ws]
            t_upr = t_upr_s[ws]
            t_wfmw = t_wfmw_s[ws]
            par = hs[0] % 2
            base = 64 * par
            pis = [(g * 16 + h) // 2 for h in hs]
            ics = [h // 2 for h in hs]
            kpar = ktm[:, g, :].rearrange("p (pr two f) -> p pr two f",
                                          two=2, f=64)[:, ics[0]:ics[0] + 4,
                                                       par, :]
            vpar = vtm[:, g, :].rearrange("p (pr two f) -> p pr two f",
                                          two=2, f=64)[:, ics[0]:ics[0] + 4,
                                                       par, :]

            pt, ptr = bpool.alloc()
            for i in range(4):
                kb.op("pe", lambda e, pt=pt, i=i, h=hs[i]:
                      e.transpose(pt[base:base + 64, i * 128:(i + 1) * 128],
                                  ktm[:, g, h * 64:(h + 1) * 64],
                                  identbf[:, :]),
                      reads=[t_ktm[g], t_consts], writes=[ptr], tag="dKt")
            kb.op("ac", lambda e, pt=pt:
                  e.copy(knfm[base:base + 64, 0:512], pt[base:base + 64, 0:512]),
                  reads=[ptr], writes=[t_knfm], tag="dKe")
            yield

            def knsl(i):
                return knfm[base:base + 64, i * 128:(i + 1) * 128]

            ptG, ptrG = fpool.alloc()
            for i in range(4):
                kb.op("pe", lambda e, ptG=ptG, i=i:
                      e.matmul(ptG[:, i * 128:(i + 1) * 128], knsl(i), knsl(i),
                               start=True, stop=True),
                      reads=[t_knfm], writes=[ptrG], tag="dG")
            for i in range(4):
                kb.op("dv", lambda e, ptG=ptG, i=i, h=hs[i]:
                      e.scalar_tensor_tensor(Qb[0][:, i * 128:(i + 1) * 128],
                                             ptG[:, i * 128:(i + 1) * 128],
                                             btm[:, g, h:h + 1],
                                             triu1s[:, 0:128],
                                             ALU.mult, ALU.mult),
                      reads=[ptrG, t_btm[g], t_consts], writes=[t_Q[0]],
                      tag="dBt")
            yield

            ptP, ptrP = bpool.alloc()
            for i in range(4):
                kb.op("pe", lambda e, ptP=ptP, i=i:
                      e.transpose(ptP[:, i * 128:(i + 1) * 128],
                                  Qb[0][:, i * 128:(i + 1) * 128],
                                  identbf[:, :]),
                      reads=[t_Q[0], t_consts], writes=[ptrP], tag="dPt")
            kb.op("ac", lambda e, ptP=ptP: e.copy(Pb[0][:, :], ptP[:, 0:512]),
                  reads=[ptrP], writes=[t_P[0]], tag="dPe")
            yield

            ptA, ptrA = fpool.alloc()
            for i in range(4):
                h = hs[i]
                kb.op("pe", lambda e, ptA=ptA, i=i, h=h:
                      e.matmul(ptA[:, i * 128:i * 128 + 64],
                               Qb[0][:, i * 128:(i + 1) * 128],
                               ktm[:, g, h * 64:(h + 1) * 64],
                               start=True, stop=True),
                      reads=[t_Q[0], t_ktm[g]], writes=[ptrA], tag="dA0k")
                kb.op("pe", lambda e, ptA=ptA, i=i, h=h:
                      e.matmul(ptA[:, i * 128 + 64:(i + 1) * 128],
                               Qb[0][:, i * 128:(i + 1) * 128],
                               vtm[:, g, h * 64:(h + 1) * 64],
                               start=True, stop=True),
                      reads=[t_Q[0], t_vtm[g]], writes=[ptrA], tag="dA0v")
            kb.op("dv", lambda e, ptA=ptA:
                  e.tensor_sub(
                      Rb[0][:, :].rearrange("p (c f) -> p c f", c=4)[:, :, 0:64],
                      kpar,
                      ptA[:, 0:512].rearrange("p (c f) -> p c f", c=4)[:, :, 0:64]),
                  reads=[ptrA, t_ktm[g]], writes=[t_R[0]], tag="dA0sk")
            kb.op("dv", lambda e, ptA=ptA:
                  e.tensor_sub(
                      Rb[0][:, :].rearrange("p (c f) -> p c f", c=4)[:, :, 64:128],
                      vpar,
                      ptA[:, 0:512].rearrange("p (c f) -> p c f", c=4)[:, :, 64:128]),
                  reads=[ptrA, t_vtm[g]], writes=[t_R[0]], tag="dA0sv")
            yield

            rcur, pcur, qcur = 0, 0, 0
            for j in range(1, 7):
                pn, qn = 1 - pcur, 1 - qcur
                if j < 6:
                    ptS, ptrS = fpool.alloc()
                    for i in range(4):
                        kb.op("pe", lambda e, ptS=ptS, i=i, qc=qcur, pc=pcur:
                              e.matmul(ptS[:, i * 128:(i + 1) * 128],
                                       Qb[qc][:, i * 128:(i + 1) * 128],
                                       Pb[pc][:, i * 128:(i + 1) * 128],
                                       start=True, stop=True),
                              reads=[t_Q[qcur], t_P[pcur]], writes=[ptrS],
                              tag="dSP")
                    kb.op("ac", lambda e, ptS=ptS, pn=pn:
                          e.copy(Pb[pn][:, :], ptS[:, 0:512]),
                          reads=[ptrS], writes=[t_P[pn]], tag="dSPe")
                    yield
                ptS2, ptrS2 = fpool.alloc()
                for i in range(4):
                    kb.op("pe", lambda e, ptS2=ptS2, i=i, qc=qcur, pc=pcur:
                          e.matmul(ptS2[:, i * 128:(i + 1) * 128],
                                   Pb[pc][:, i * 128:(i + 1) * 128],
                                   Qb[qc][:, i * 128:(i + 1) * 128],
                                   start=True, stop=True),
                          reads=[t_Q[qcur], t_P[pcur]], writes=[ptrS2],
                          tag="dSQ")
                kb.op("ac", lambda e, ptS2=ptS2, qn=qn:
                      e.copy(Qb[qn][:, :], ptS2[:, 0:512]),
                      reads=[ptrS2], writes=[t_Q[qn]], tag="dSQe")
                yield
                ptA2, ptrA2 = fpool.alloc()
                rn = 1 - rcur
                for i in range(4):
                    kb.op("pe", lambda e, ptA2=ptA2, i=i, qn=qn, rc=rcur:
                          e.matmul(ptA2[:, i * 128:(i + 1) * 128],
                                   Qb[qn][:, i * 128:(i + 1) * 128],
                                   Rb[rc][:, i * 128:(i + 1) * 128],
                                   start=True, stop=True),
                          reads=[t_Q[qn], t_R[rcur]], writes=[ptrA2], tag="dAp")
                kb.op("dv", lambda e, ptA2=ptA2, rn=rn, rc=rcur:
                      e.tensor_add(Rb[rn][:, :], Rb[rc][:, :], ptA2[:, 0:512]),
                      reads=[ptrA2, t_R[rcur]], writes=[t_R[rn]], tag="dApe")
                yield
                rcur = rn
                if j < 6:
                    pcur = pn
                qcur = qn

            ptW, ptrW = bpool.alloc()
            for i in range(4):
                kb.op("pe", lambda e, ptW=ptW, i=i, rc=rcur:
                      e.transpose(ptW[base:base + 64, i * 128:(i + 1) * 128],
                                  Rb[rc][:, i * 128:i * 128 + 64],
                                  identbf[:, :]),
                      reads=[t_R[rcur], t_consts], writes=[ptrW], tag="dWt")
            kb.op("ac", lambda e, ptW=ptW:
                  e.copy(wfmw[base:base + 64, 0:512], ptW[base:base + 64, 0:512]),
                  reads=[ptrW], writes=[t_wfmw], tag="dWe")
            yield

            ptWS, ptrWS = fpool.alloc()
            for i in range(4):
                kb.op("pe", lambda e, ptWS=ptWS, i=i, pi=pis[i]:
                      e.matmul(ptWS[:, i * 64:(i + 1) * 64],
                               wfmw[base:base + 64, i * 128:(i + 1) * 128],
                               Sbf[base:base + 64, pi * 64:(pi + 1) * 64],
                               start=True, stop=True),
                      reads=[t_wfmw, t_Sbf[pis[i] * 2 + par]],
                      writes=[ptrWS], tag="dWS")
            kb.op("dv", lambda e, ptWS=ptWS, rc=rcur:
                  e.tensor_sub(
                      uprs[:, 0:256].rearrange("p (c f) -> p c f", c=4),
                      Rb[rc][:, :].rearrange("p (c f) -> p c f",
                                             c=4)[:, :, 64:128],
                      ptWS[:, 0:256].rearrange("p (c f) -> p c f", c=4)),
                  reads=[ptrWS, t_R[rcur]], writes=[t_upr], hazard=True,
                  tag="dUs")
            for i in range(4):
                kb.op("dv", lambda e, i=i, h=hs[i]:
                      e.tensor_scalar(uprs[:, i * 64:(i + 1) * 64],
                                      uprs[:, i * 64:(i + 1) * 64],
                                      btm[:, g, h:h + 1], None, ALU.mult),
                      reads=[t_btm[g], t_upr], writes=[t_upr],
                      hazard=True, tag="dUb")
            yield

            ptT, ptrT = fpool.alloc()
            for i in range(4):
                kb.op("pe", lambda e, ptT=ptT, i=i, ic=ics[i]:
                      e.matmul(ptT[:, i * 128:(i + 1) * 128],
                               knsl(i),
                               qfm[base:base + 64, ic, g * 128:(g + 1) * 128],
                               start=True, stop=True),
                      reads=[t_knfm, t_qfm[ics[i]]], writes=[ptrT], tag="dT")
            for i in range(4):
                kb.op("dv", lambda e, ptT=ptT, i=i:
                      e.tensor_mul(ttrs[:, i * 128:(i + 1) * 128],
                                   ptT[:, i * 128:(i + 1) * 128],
                                   triis[:, 0:128]),
                      reads=[ptrT, t_consts], writes=[t_ttr], tag="dTm")
            yield

            ptO, ptrO = fpool.alloc()
            for i in range(4):
                ic = ics[i]
                kb.op("pe", lambda e, ptO=ptO, i=i, pi=pis[i], ic=ic:
                      e.matmul(ptO[base:base + 64, i * 128:(i + 1) * 128],
                               Sbf[base:base + 64, pi * 64:(pi + 1) * 64],
                               qfm[base:base + 64, ic, g * 128:(g + 1) * 128],
                               start=True, stop=False),
                      reads=[t_Sbf[pis[i] * 2 + par], t_qfm[ic]],
                      writes=[ptrO], tag="dO1")
                kb.op("pe", lambda e, ptO=ptO, i=i:
                      e.matmul(ptO[base:base + 64, i * 128:(i + 1) * 128],
                               uprs[:, i * 64:(i + 1) * 64],
                               ttrs[:, i * 128:(i + 1) * 128],
                               start=False, stop=True),
                      reads=[t_upr, t_ttr], writes=[ptrO], tag="dO2")
            for i in range(4):
                ic = ics[i]
                kb.op("ac", lambda e, ptO=ptO, i=i, ic=ic:
                      e.copy(qfm[base:base + 64, ic, g * 128:(g + 1) * 128],
                             ptO[base:base + 64, i * 128:(i + 1) * 128]),
                      reads=[ptrO], writes=[t_qfm[ic]], tag="dOe")
            yield

            ptD, ptrD = fpool.alloc()
            for i in range(4):
                h = hs[i]
                kb.op("pe", lambda e, ptD=ptD, i=i, h=h:
                      e.matmul(ptD[base:base + 64, i * 64:(i + 1) * 64],
                               ktm[:, g, h * 64:(h + 1) * 64],
                               uprs[:, i * 64:(i + 1) * 64],
                               start=True, stop=True),
                      reads=[t_ktm[g], t_upr], writes=[ptrD], tag="dDS")
            s0 = pis[0] * 64
            kb.op("dv", lambda e, ptD=ptD, s0=s0:
                  e.tensor_add(S32[base:base + 64, s0:s0 + 256],
                               S32[base:base + 64, s0:s0 + 256],
                               ptD[base:base + 64, 0:256]),
                  reads=[ptrD] + [t_S32[pi * 2 + par] for pi in pis],
                  writes=[t_S32[pi * 2 + par] for pi in pis], tag="dDe")
            kb.op("ac", lambda e, s0=s0:
                  e.copy(Sbf[base:base + 64, s0:s0 + 256],
                         S32[base:base + 64, s0:s0 + 256]),
                  reads=[t_S32[pi * 2 + par] for pi in pis],
                  writes=[t_Sbf[pi * 2 + par] for pi in pis], tag="dDc")


        for ft in range(N_FFN):
            for at in range(TILES_PER_FFN):
                tt = ft * TILES_PER_FFN + at
                W0 = tt * C
                attn_fence = kb.fence() if at == 0 else []

                for b in range(B_LOC):
                    gg = at * 4 + b
                    dma("sy", x1[:, gg, :], x_d[b, W0:W0 + C, :], f"xl{tt}",
                        writes=[t_x1[gg]])

                if not SKIP_ATTN:
                    wload(wbufA, t_wbufA, wq_d, f"wq{tt}", extra=attn_fence, sidx=0, tt=tt)
                    wload(wbufB, t_wbufB, wk_d, f"wk{tt}", extra=attn_fence, sidx=1, tt=tt)
                    dma("gp", wbetabf[:, :, :],
                        wbeta_d.rearrange("(dc p) h -> p dc h", p=128),
                        f"wb{tt}", writes=[t_wbetabf], deps=attn_fence)

                    if attn_fence:
                        kb.op("dv", lambda e: e.engine_nop(), deps=attn_fence,
                              tag="fdv")

                    for b in range(B_LOC):
                        emit_ln(at * 4 + b, xnfm, [t_xnfm], b * 128, f"l1_{tt}{b}")

                    for ic in range(8):
                        pt, ptr = fpool.alloc()
                        for dc in range(8):
                            kb.op("pe", lambda e, pt=pt, ic=ic, dc=dc:
                                  e.matmul(pt[:, 0:512],
                                           wbufA[:, dc, ic * 128:(ic + 1) * 128],
                                           xnfm[:, dc, :],
                                           start=(dc == 0), stop=(dc == 7)),
                                  reads=[t_wbufA, t_xnfm], writes=[ptr], tag="qmm")
                        kb.op("ac", lambda e, pt=pt, ic=ic:
                              e.copy(qfm[:, ic, :], pt[:, 0:512]),
                              reads=[ptr], writes=[t_qfm[ic]], tag="qev")

                    def emit_knorm(g):
                        rn = rnorm[:, g * 16:(g + 1) * 16]
                        trn = t_rnorm[g]
                        for half in range(2):
                            kb.op("ac", lambda e, g=g, half=half:
                                  e.activation(attsc[:, 0:512],
                                               ktm[:, g,
                                                   half * 512:(half + 1) * 512],
                                               AF.Square),
                                  reads=[t_ktm[g]], writes=[t_attsc],
                                  tag="ksq")
                            kb.op("dv", lambda e, rn=rn, half=half:
                                  e.tensor_reduce(rn[:, half * 8:half * 8 + 8],
                                                  attsc[:, 0:512].rearrange(
                                                      "p (h d) -> p h d", h=8),
                                                  AX.X, ALU.add),
                                  reads=[t_attsc], writes=[trn], hazard=True,
                                  tag="krd")
                        kb.op("dv", lambda e, rn=rn:
                              e.tensor_scalar(rn[:, 0:16], rn[:, 0:16],
                                              1e-24, None, ALU.max),
                              reads=[trn], writes=[trn], hazard=True,
                              tag="kcl")
                        kb.op("ac", lambda e, rn=rn:
                              e.activation(rn[:, 0:16], rn[:, 0:16], AF.Ln),
                              reads=[trn], writes=[trn], hazard=True, tag="kln")
                        kb.op("dv", lambda e, rn=rn:
                              e.tensor_scalar(rn[:, 0:16], rn[:, 0:16],
                                              -0.5, None, ALU.mult),
                              reads=[trn], writes=[trn], hazard=True, tag="kmh")
                        kb.op("ac", lambda e, rn=rn:
                              e.activation(rn[:, 0:16], rn[:, 0:16], AF.Exp),
                              reads=[trn], writes=[trn], hazard=True, tag="kex")
                        for h in range(16):
                            kb.op("dv", lambda e, g=g, h=h, rn=rn:
                                  e.tensor_scalar(ktm[:, g, h * 64:(h + 1) * 64],
                                                  ktm[:, g, h * 64:(h + 1) * 64],
                                                  rn[:, h:h + 1], None,
                                                  ALU.mult),
                                  reads=[trn, t_ktm[g]], writes=[t_ktm[g]],
                                  tag="knm")

                    for g in range(4):
                        for half in range(2):
                            pt, ptr = fpool.alloc()
                            for dc in range(8):
                                kb.op("pe", lambda e, pt=pt, g=g, half=half, dc=dc:
                                      e.matmul(pt[:, 0:512],
                                               xnfm[:, dc, g * 128:(g + 1) * 128],
                                               wbufB[:, dc,
                                                     half * 512:(half + 1) * 512],
                                               start=(dc == 0), stop=(dc == 7)),
                                      reads=[t_wbufB, t_xnfm], writes=[ptr],
                                      tag="kmm")
                            kb.op("ac", lambda e, pt=pt, g=g, half=half:
                                  e.copy(ktm[:, g, half * 512:(half + 1) * 512],
                                         pt[:, 0:512]),
                                  reads=[ptr], writes=[t_ktm[g]], tag="kev")
                        if not SKIP_DN:
                            emit_knorm(g)

                    wload(wbufA, t_wbufA, wv_d, f"wv{tt}", sidx=2, tt=tt)
                    for g in range(4):
                        for half in range(2):
                            pt, ptr = fpool.alloc()
                            for dc in range(8):
                                kb.op("pe", lambda e, pt=pt, g=g, half=half, dc=dc:
                                      e.matmul(pt[:, 0:512],
                                               xnfm[:, dc, g * 128:(g + 1) * 128],
                                               wbufA[:, dc,
                                                     half * 512:(half + 1) * 512],
                                               start=(dc == 0), stop=(dc == 7)),
                                      reads=[t_wbufA, t_xnfm], writes=[ptr],
                                      tag="vmm")
                            kb.op("ac", lambda e, pt=pt, g=g, half=half:
                                  e.copy(vtm[:, g, half * 512:(half + 1) * 512],
                                         pt[:, 0:512]),
                                  reads=[ptr], writes=[t_vtm[g]], tag="vev")

                    wload(wbufB, t_wbufB, wout_d, f"wo{tt}", sidx=3, tt=tt)

                    for g in range(4):
                        pt, ptr = fpool.alloc()
                        for dc in range(8):
                            kb.op("pe", lambda e, pt=pt, g=g, dc=dc:
                                  e.matmul(pt[0:16, 0:128],
                                           wbetabf[:, dc, :],
                                           xnfm[:, dc, g * 128:(g + 1) * 128],
                                           start=(dc == 0), stop=(dc == 7)),
                                  reads=[t_wbetabf, t_xnfm], writes=[ptr],
                                  tag="bmm")
                        kb.op("ac", lambda e, pt=pt, g=g:
                              e.activation(bsig[:, g * 128:(g + 1) * 128],
                                           pt[0:16, 0:128], AF.Sigmoid,
                                           bias=bbetas[:, 0:1]),
                              reads=[ptr, t_cstf], writes=[t_bsig], tag="bsg")
                        pt2, ptr2 = fpool.alloc()
                        kb.op("pe", lambda e, pt2=pt2, g=g:
                              e.transpose(pt2[0:128, 0:16],
                                          bsig[:, g * 128:(g + 1) * 128],
                                          identf[0:16, 0:16]),
                              reads=[t_bsig, t_consts], writes=[ptr2], tag="btr")
                        kb.op("dv", lambda e, pt2=pt2, g=g:
                              e.tensor_copy(btm[:, g, :], pt2[0:128, 0:16]),
                              reads=[ptr2], writes=[t_btm[g]], hazard=True,
                          tag="btv")

                    wave_list = []
                    for g in range(4 if not SKIP_DN else 0):
                        for hb in range(2):
                            for par in range(2):
                                wave_list.append(
                                    (g, [par + 2 * (4 * hb + i)
                                         for i in range(4)]))
                    if wave_list:
                        gens = [emit_wave(*wave_list[0], tt, 0),
                                emit_wave(*wave_list[1], tt, 1)]
                        for _ in range(5):
                            next(gens[0])
                        nxt_w = 2
                        turn = 0
                        while gens[0] is not None or gens[1] is not None:
                            i = turn % 2
                            if gens[i] is not None:
                                try:
                                    next(gens[i])
                                except StopIteration:
                                    if nxt_w < len(wave_list):
                                        gens[i] = emit_wave(
                                            *wave_list[nxt_w], tt, i)
                                        nxt_w += 1
                                    else:
                                        gens[i] = None
                            turn += 1

                    for dc in range(8):
                        pt, ptr = fpool.alloc()
                        for ic in range(8):
                            kb.op("pe", lambda e, pt=pt, dc=dc, ic=ic:
                                  e.matmul(pt[:, 0:512],
                                           wbufB[:, ic, dc * 128:(dc + 1) * 128],
                                           qfm[:, ic, :],
                                           start=(ic == 0), stop=(ic == 7)),
                                  reads=[t_wbufB, t_qfm[ic]], writes=[ptr],
                                  tag="womm")
                        kb.op("dv", lambda e, pt=pt:
                              e.tensor_copy(attsc[:, 0:512], pt[:, 0:512]),
                              reads=[ptr], writes=[t_attsc], tag="woev")
                        pt2, ptr2 = fpool.alloc()
                        for j in range(4):
                            kb.op("pe", lambda e, pt2=pt2, j=j:
                                  e.transpose(pt2[:, j * 128:(j + 1) * 128],
                                              attsc[:, j * 128:(j + 1) * 128],
                                              identf[:, :]),
                                  reads=[t_attsc, t_consts], writes=[ptr2],
                                  tag="wotr")
                        kb.op("dv", lambda e, pt2=pt2, dc=dc, at=at:
                              e.tensor_tensor(
                                  x1[:, at * 4:(at + 1) * 4,
                                     dc * 128:(dc + 1) * 128],
                                  x1[:, at * 4:(at + 1) * 4,
                                     dc * 128:(dc + 1) * 128],
                                  pt2[:, 0:512].rearrange("p (j t) -> p j t", j=4),
                                  ALU.add),
                              reads=[ptr2] + [t_x1[at * 4 + j] for j in range(4)],
                              writes=[t_x1[at * 4 + j] for j in range(4)],
                              tag="wors")

                for b in range(B_LOC):
                    ggf = at * 4 + b
                    emit_ln(ggf, xn2, t_xn2, ggf * 128, f"l2_{tt}{b}")
                    pt, ptr = fpool.alloc()
                    for dc in range(8):
                        kb.op("pe", lambda e, pt=pt, dc=dc, ggf=ggf:
                              e.matmul(pt[0:128, 0:8],
                                       xn2[:, dc, ggf * 128:(ggf + 1) * 128],
                                       cstb[:, 1160 + dc * 8:1168 + dc * 8],
                                       start=(dc == 0), stop=(dc == 7)),
                              reads=[t_xn2[dc // 4], t_cstb], writes=[ptr],
                              tag="gmm")
                    kb.op("dv", lambda e, pt=pt:
                          e.tensor_add(gcols[:, 8:16], pt[0:128, 0:8],
                                       cols[:, 8:16]),
                          reads=[ptr, t_cols], writes=[t_gcols],
                          hazard=True, tag="glg")
                    kb.op("dv", lambda e:
                          e.reduce_max(gcols[:, 0:1], gcols[:, 8:16], AX.X,
                                       negate=True),
                          reads=[t_gcols], writes=[t_gcols], hazard=True,
                          tag="gnm")
                    kb.op("ac", lambda e:
                          e.activation(gp_t[:, :], gcols[:, 8:16], AF.Exp,
                                       bias=gcols[:, 0:1]),
                          reads=[t_gcols], writes=[t_gp], tag="gex")
                    kb.op("dv", lambda e:
                          e.tensor_scalar(gind[:, :], gp_t[:, :], 1.0, None,
                                          ALU.is_equal),
                          reads=[t_gp], writes=[t_gind], hazard=True, tag="geq")
                    kb.op("dv", lambda e:
                          e.tensor_sub(gind[:, :], gp_t[:, :], gind[:, :]),
                          reads=[t_gp, t_gind], writes=[t_gind], hazard=True, tag="gsb")
                    kb.op("dv", lambda e:
                          e.reduce_max(gcols[:, 1:2], gind[:, :], AX.X),
                          reads=[t_gind], writes=[t_gcols], hazard=True,
                          tag="gm2")
                    kb.op("dv", lambda e:
                          e.tensor_scalar(gcols[:, 2:3], gcols[:, 1:2], 1.0,
                                          None, ALU.add),
                          reads=[t_gcols], writes=[t_gcols], hazard=True,
                          tag="gdn")
                    kb.op("dv", lambda e:
                          e.reciprocal(gcols[:, 3:4], gcols[:, 2:3]),
                          reads=[t_gcols], writes=[t_gcols], hazard=True,
                          tag="grc")
                    kb.op("dv", lambda e:
                          e.tensor_scalar(gind[:, :], gp_t[:, :],
                                          gcols[:, 1:2], None, ALU.is_ge),
                          reads=[t_gp, t_gcols], writes=[t_gind], hazard=True, tag="gge")
                    kb.op("dv", lambda e, ggf=ggf:
                          e.scalar_tensor_tensor(
                              wcols[:, ggf * 8:(ggf + 1) * 8],
                              gp_t[:, :], gcols[:, 3:4], gind[:, :],
                              ALU.mult, ALU.mult),
                          reads=[t_gp, t_gcols, t_gind], writes=[t_wcols],
                          tag="gw")
                    pt3, ptr3 = fpool.alloc()
                    kb.op("pe", lambda e, pt3=pt3, ggf=ggf:
                          e.transpose(pt3[0:8, 0:128],
                                      wcols[:, ggf * 8:(ggf + 1) * 8],
                                      identf[:, :]),
                          reads=[t_wcols, t_consts], writes=[ptr3], tag="gwt")
                    kb.op("ac", lambda e, pt3=pt3, ggf=ggf:
                          e.copy(wfm[:, ggf * 128:(ggf + 1) * 128],
                                 pt3[0:8, 0:128]),
                          reads=[ptr3], writes=[t_wfm], tag="gwf")

            ffn_fence = kb.fence()
            for gg in range(NG):
                for dh in range(2):
                    pt, ptr = fpool.alloc()
                    kb.op("pe", lambda e, pt=pt, gg=gg, dh=dh:
                          e.matmul(pt[:, 0:512],
                                   wfm[:, gg * 128:(gg + 1) * 128],
                                   b2bf[:, dh * 512:(dh + 1) * 512],
                                   start=True, stop=True),
                          reads=[t_wfm, t_cstb], writes=[ptr], tag="b2m")
                    kb.op("dv", lambda e, pt=pt, gg=gg, dh=dh:
                          e.tensor_tensor(x1[:, gg, dh * 512:(dh + 1) * 512],
                                          x1[:, gg, dh * 512:(dh + 1) * 512],
                                          pt[:, 0:512], ALU.add),
                          reads=[ptr, t_x1[gg]], writes=[t_x1[gg]], tag="b2e")

            for e_i in range(NE if not SKIP_FFN else 0):
                for fh in range(2):
                    first = (e_i == 0 and fh == 0)
                    w1r = w1_d[e_i].rearrange("(dc p) f -> p dc f", p=128)
                    f0 = fh * 2048
                    if ft == 0:
                        for dc4 in range(2):
                            for s2 in range(2):
                                dma("gp",
                                    w1h[:, dc4 * 4:(dc4 + 1) * 4,
                                        s2 * 1024:(s2 + 1) * 1024],
                                    w1r[:, dc4 * 4:(dc4 + 1) * 4,
                                        f0 + s2 * 1024:f0 + (s2 + 1) * 1024],
                                    "w1", writes=[t_w1h],
                                    deps=(ffn_fence if first else []))
                        dma("sy", w1s_d[e_i, fh], w1h[:, :, :], "w1st",
                            reads=[t_w1h])
                    else:
                        dma("sy", w1h[:, :, :], w1s_d[e_i, fh], "w1ld",
                            writes=[t_w1h],
                            deps=(ffn_fence if first else []))
                    w2r = w2_d[e_i].rearrange("(fc p) d -> p fc d", p=128)
                    fc0 = fh * 16

                    for tc in range(4):
                        tok0 = tc * 512
                        for fc in range(16):
                            pt, ptr = fpool.alloc(subset=(4, 5))
                            for dc in range(8):
                                kb.op("pe", lambda e, pt=pt, fc=fc, dc=dc,
                                      tok0=tok0:
                                      e.matmul(pt[:, 0:512],
                                               w1h[:, dc,
                                                   fc * 128:(fc + 1) * 128],
                                               xn2[:, dc, tok0:tok0 + 512],
                                               start=(dc == 0), stop=(dc == 7)),
                                      reads=[t_w1h, t_xn2[dc // 4]],
                                      writes=[ptr], tag="hmm")
                            fcg = fh * 16 + fc
                            kb.op("ac", lambda e, pt=pt, fc=fc, fcg=fcg,
                                  e_i=e_i:
                                  e.activation(hg[:, fc, :], pt[:, 0:512],
                                               AF.Gelu,
                                               bias=cstf[:, 1 + fcg * 8 + e_i:
                                                         2 + fcg * 8 + e_i]),
                                  reads=[ptr, t_cstf], writes=[t_hg],
                                  tag="gel")
                        if tc == 0:
                            if ft == 0:
                                for q in range(4):
                                    dma("gp", w2h[:, q * 4:(q + 1) * 4, :],
                                        w2r[:, fc0 + q * 4:fc0 + (q + 1) * 4, :],
                                        "w2", writes=[t_w2h],
                                        deps=(ffn_fence if first else []))
                                dma("sy", w2s_d[e_i, fh], w2h[:, :, :], "w2st",
                                    reads=[t_w2h])
                            else:
                                dma("sy", w2h[:, :, :], w2s_d[e_i, fh], "w2ld",
                                    writes=[t_w2h],
                                    deps=(ffn_fence if first else []))
                        for gl in range(4):
                            gg = tc * 4 + gl
                            sub = (0, 1) if gl % 2 == 0 else (2, 3)
                            ptA, ptrA = fpool.alloc(subset=sub)
                            ptB, ptrB = fpool.alloc(subset=sub)
                            for fc in range(16):
                                kb.op("pe", lambda e, ptA=ptA, fc=fc, gl=gl:
                                      e.matmul(ptA[:, 0:512],
                                               hg[:, fc,
                                                  gl * 128:(gl + 1) * 128],
                                               w2h[:, fc, 0:512],
                                               start=(fc == 0),
                                               stop=(fc == 15),
                                               skip_group_check=True),
                                      reads=[t_hg, t_w2h], writes=[ptrA],
                                      tag="ymA")
                                kb.op("pe", lambda e, ptB=ptB, fc=fc, gl=gl:
                                      e.matmul(ptB[:, 0:512],
                                               hg[:, fc,
                                                  gl * 128:(gl + 1) * 128],
                                               w2h[:, fc, 512:1024],
                                               start=(fc == 0),
                                               stop=(fc == 15),
                                               skip_group_check=True),
                                      reads=[t_hg, t_w2h], writes=[ptrB],
                                      tag="ymB")
                            kb.op("dv", lambda e, ptA=ptA, gg=gg, e_i=e_i:
                                  e.scalar_tensor_tensor(
                                      x1[:, gg, 0:512], ptA[:, 0:512],
                                      wcols[:, gg * 8 + e_i:gg * 8 + e_i + 1],
                                      x1[:, gg, 0:512], ALU.mult, ALU.add),
                                  reads=[ptrA, t_wcols, t_x1[gg]],
                                  writes=[t_x1[gg]], tag="yeA")
                            kb.op("dv", lambda e, ptB=ptB, gg=gg, e_i=e_i:
                                  e.scalar_tensor_tensor(
                                      x1[:, gg, 512:1024], ptB[:, 0:512],
                                      wcols[:, gg * 8 + e_i:gg * 8 + e_i + 1],
                                      x1[:, gg, 512:1024], ALU.mult, ALU.add),
                                  reads=[ptrB, t_wcols, t_x1[gg]],
                                  writes=[t_x1[gg]], tag="yeB")

            for gg in range(NG):
                at_, b = gg // 4, gg % 4
                W0 = (ft * TILES_PER_FFN + at_) * C
                dma("gp", out_d[b, W0:W0 + C, :], x1[:, gg, :], f"st{ft}",
                    reads=[t_x1[gg]])

        sy_total = sum(16 for o in kb.ops if o.eng == "sy" and o.dma)
        gp_total = sum(16 for o in kb.ops if o.eng == "gp" and o.dma)
        kb.op("sy", lambda e, v=sy_total: e.wait_ge(sems["sy"], v), tag="fin")
        kb.op("sy", lambda e, v=gp_total: e.wait_ge(sems["gp"], v), tag="fin2")

        kb.finalize_and_emit(sems)

    return nc


_NC_CACHE = {}


def kernel(L0_attn_x, L0_attn_Wq, L0_attn_Wk, L0_attn_Wv, L0_attn_Wbeta,
           L0_attn_b_beta, L0_attn_Wout, L0_ffn_Wg, L0_ffn_bg, L0_ffn_W1,
           L0_ffn_b1, L0_ffn_W2, L0_ffn_b2):
    x = np.ascontiguousarray(np.asarray(L0_attn_x, dtype=np.float32))
    n_cores = 8
    if "nc" not in _NC_CACHE:
        _NC_CACHE["nc"] = build_nc()
    nc = _NC_CACHE["nc"]

    b1 = np.asarray(L0_ffn_b1, dtype=np.float32)
    cst_f = np.zeros((128, 257), dtype=np.float32)
    cst_f[0:H, 0] = np.asarray(L0_attn_b_beta, dtype=np.float32).reshape(H)
    cst_f[:, 1:257] = b1.reshape(NE, 32, 128).transpose(2, 1, 0).reshape(128, 256)
    cst_b = np.zeros((128, 1224), dtype=np.float32)
    cst_b[0:NE, 0:1024] = np.asarray(L0_ffn_b2, dtype=np.float32)
    cst_b[0, 1024:1032] = np.asarray(L0_ffn_bg, dtype=np.float32).reshape(NE)
    cst_b[0, 1032:1160] = 1.0
    wg = np.asarray(L0_ffn_Wg, dtype=np.float32)
    cst_b[:, 1160:1224] = wg.reshape(8, 128, NE).transpose(1, 0, 2).reshape(128, 64)

    common = {
        "wq": np.ascontiguousarray(L0_attn_Wq, dtype=np.float32),
        "wk": np.ascontiguousarray(L0_attn_Wk, dtype=np.float32),
        "wv": np.ascontiguousarray(L0_attn_Wv, dtype=np.float32),
        "wbeta": np.ascontiguousarray(L0_attn_Wbeta, dtype=np.float32),
        "wout": np.ascontiguousarray(L0_attn_Wout, dtype=np.float32),
        "w1": np.ascontiguousarray(L0_ffn_W1, dtype=np.float32),
        "w2": np.ascontiguousarray(L0_ffn_W2, dtype=np.float32),
        "cst_f": cst_f,
        "cst_b": cst_b,
    }
    in_maps = []
    for c in range(n_cores):
        m = dict(common)
        m["x"] = np.ascontiguousarray(x[c * B_LOC:(c + 1) * B_LOC])
        in_maps.append(m)

    res = run_bass_kernel_spmd(nc, in_maps, core_ids=list(range(n_cores)))
    out = np.concatenate([np.asarray(r["out"]) for r in res.results], axis=0)
    return out.astype(np.float32)
